# version 04 (final)
# speedup vs baseline: 1.0497x; 1.0035x over previous
import contextlib
import math
import os
import numpy as np
import ml_dtypes
import concourse.bass as bass
import concourse.mybir as mybir
from concourse.bass_utils import run_bass_kernel_spmd

F32 = mybir.dt.float32
BF16 = mybir.dt.bfloat16
I32 = mybir.dt.int32
I16 = mybir.dt.int16
I8 = mybir.dt.int8
AF = mybir.ActivationFunctionType
ALU = mybir.AluOpType
AX = mybir.AxisListType
BF = ml_dtypes.bfloat16

T = int(os.environ.get("KERNEL_T", "2048"))
D = 1024
NCH = 8
TT = T // 128
NTB = T // 512
LN_EPS = 1e-5
N_LAYERS = int(os.environ.get("KERNEL_LAYERS", "4"))
ONLY = os.environ.get("KERNEL_ONLY", "")
CC_CORES = int(os.environ.get("KERNEL_CC_CORES", "8"))


REGION_OF = {}
for _r, _ks in {
    "rows": ["rows", "sint", "nsint", "cost", "n2", "zr", "z", "qn2",
             "kn2", "mxr", "mxs", "ti", "gi", "ui", "zi", "sigi", "posr", "qti",
             "l1mean", "l1a", "l1ainv", "l1r", "l1S", "l1Q"],
    "small": ["gnS", "gnQ", "gnSs", "gnQs", "gnM2", "qdsum", "qdsumn", "smq",
              "smq2", "dwc", "mxd", "mxm", "mxsl", "mxsc", "mxsr", "am1",
              "ccl", "ccs", "bhx", "rpe", "qbm", "qbr", "cc0", "mk2", "exp4",
              "rmx", "rmx1", "nrm", "l1tc", "l1b1c", "l1b1pc", "l1b2c", "l1b2pc",
              "l1m1s"],
    "sm16": ["sm16", "ET"],
    "tfA": ["tfA", "swp", "frq", "sig", "tv", "dpart", "bcZt", "l1u2", "l1r2"],
    "tfB": ["tfB", "srx", "cj", "l1np", "l1beta", "l1mu2", "l1g2pc", "l1o32",
            "l1c1", "l1c2"],
    "bcA": ["bcA", "scx", "bcZ", "ki", "l1mb", "l1ab", "l1rb", "l1rbn", "l1d",
            "l1rB", "l1A2", "l1c3"],
    "cols": ["gcr", "gcm", "gal", "gbe", "alpha0", "dcol", "ndcol", "epscol",
             "dac", "sc", "amx", "nB", "l1m1", "l1m2"],
    "ccol": ["gnw", "gnb", "bf0", "bf3", "sqk", "l1g1"],
    "b1": ["s16", "xq16", "k"],
    "Rqsl": ["qsl"], "Rksl": ["ksl"], "Rcxs": ["cxs"],
    "Rqst2": ["qst2"], "Rkst2": ["kst2"],
    "Rl1bas": ["l1bas"], "Rl1slab": ["l1slab"], "Rl1slab2": ["l1slab2"],
    "b2": ["v", "qfm", "ctx", "g16", "vsl", "qst", "kst", "oq", "Mps",
           "l1mu16", "l1nrow", "l1bln16", "l1b2ln16"],
    "RvslL1": ["vslL1"], "RvslL2": ["vslL2"], "Rl1ctx": ["l1ctx"],
    "Rl1bufb": ["l1bufb"],
    "wbufB": ["wbufB"], "wbufC": ["wbufC"],
    "xn16": ["xn16", "xnq", "wdn", "xng", "l1h"],
    "wbuf": ["wbuf"],
    "brow": ["brow"],
}.items():
    for _k in _ks:
        REGION_OF[_k] = _r


class Prog:
    ENGINES = ("tensor", "vector", "scalar", "gpsimd", "sync")
    SEMS = ("tensor", "vector", "scalar", "gpsimd", "qsync", "qgp", "qout", "cc")

    def __init__(self, nc, es):
        self.nc = nc
        self.ops = {e: [] for e in self.ENGINES}
        self.sem = {s: es.enter_context(nc.semaphore("S_" + s)) for s in self.SEMS}
        self.count = {s: 0 for s in self.SEMS}
        self.writer = {}
        self.readers = {}
        self.seen = {e: {} for e in self.ENGINES}
        self.pending = {}

    def emit(self, engine, fn, reads=(), writes=(), queue=None, step=None):
        sem_name = queue if queue else engine
        if step is None:
            step = 16 if queue else 1
        reads = list(reads) + [("R", REGION_OF[k[0]]) for k in reads
                               if isinstance(k, tuple) and k and k[0] in REGION_OF]
        writes = list(writes) + [("R", REGION_OF[k[0]]) for k in writes
                                 if isinstance(k, tuple) and k and k[0] in REGION_OF]
        needs = {}
        for key in reads:
            w = self.writer.get(key)
            if w and needs.get(w[0], 0) < w[1]:
                needs[w[0]] = w[1]
        for key in writes:
            w = self.writer.get(key)
            if w and needs.get(w[0], 0) < w[1]:
                needs[w[0]] = w[1]
            for s, c in self.readers.get(key, {}).items():
                if needs.get(s, 0) < c:
                    needs[s] = c
        seen = self.seen[engine]
        waits = []
        if queue:
            v = self.pending.get(sem_name, 0)
            if v > seen.get(sem_name, 0):
                waits.append((sem_name, v))
                seen[sem_name] = v
        for s in sorted(needs):
            if queue and s == sem_name:
                continue
            c = needs[s]
            if s.startswith("q"):
                c = self.count[s]
            if seen.get(s, 0) < c:
                waits.append((s, c))
                seen[s] = c
            if s.startswith("q"):
                self.pending[s] = max(self.pending.get(s, 0), c)
        my = self.count[sem_name] + step
        self.count[sem_name] = my
        sems = self.sem

        def closure(eng, waits=waits, fn=fn, sem=sems[sem_name], step=step):
            for s, c in waits:
                eng.wait_ge(sems[s], c)
            fn(eng).then_inc(sem, step)
        self.ops[engine].append(closure)
        for key in reads:
            self.readers.setdefault(key, {})[sem_name] = my
        for key in writes:
            self.writer[key] = (sem_name, my)
            self.readers[key] = {}
        return my

    def full_fence(self):
        snap = {s: c for s, c in self.count.items() if c}
        sems = self.sem
        for e in self.ENGINES:
            def closure(eng, snap=snap, e=e):
                for s, c in snap.items():
                    if s != e:
                        eng.wait_ge(sems[s], c)
            self.ops[e].append(closure)
            self.seen[e] = dict(snap)
        self.writer = {}
        self.readers = {}
        for s, c in snap.items():
            if s.startswith("q"):
                self.pending[s] = max(self.pending.get(s, 0), c)

    def wait_all(self, engine):
        snap = dict(self.count)
        sems = self.sem

        def closure(eng, snap=snap):
            for s, c in snap.items():
                if c:
                    eng.wait_ge(sems[s], c)
        self.ops[engine].append(closure)

    def finalize(self, block):
        for name in self.ENGINES:
            ops = self.ops[name]
            if not ops:
                continue

            def make(ops):
                def _(eng):
                    for fn in ops:
                        fn(eng)
                return _
            getattr(block, name)(make(ops))


def _t(a):
    return np.ascontiguousarray(np.asarray(a, dtype=np.float32).T)


def _f32(a):
    return np.ascontiguousarray(np.asarray(a, dtype=np.float32))


def host_prep(inputs):
    g = {}
    g["l0_wqkv_t"] = _t(inputs["L0_attn_Wqkv_w"])
    g["l0_bqkv_row"] = _f32(inputs["L0_attn_Wqkv_b"]).reshape(1, 3072)
    g["l0_gnw_col"] = np.ascontiguousarray(_f32(inputs["L0_attn_gn_w"]).reshape(8, 128).T)
    g["l0_gnb_col"] = np.ascontiguousarray(_f32(inputs["L0_attn_gn_b"]).reshape(8, 128).T)
    g["l0_wout_t"] = _t(inputs["L0_attn_out_w"])
    g["l0_bout_row"] = _f32(inputs["L0_attn_out_b"]).reshape(1, D)
    g["l0_wf_t"] = _t(inputs["L0_ffn_weight"])
    g["l0_alpha"] = _f32(inputs["L0_ffn_alpha"]).reshape(1, 1)
    g["l0_bf_col"] = np.ascontiguousarray(_f32(inputs["L0_ffn_bias"]).reshape(8, 128).T)
    wspan = _f32(inputs["L1_attn_W_span"])
    wsp = np.zeros((64, 1024), dtype=np.float32)
    wsp[0:16] = wspan[0::2]
    wsp[32:48] = wspan[1::2]
    g["l1_wspan_t"] = _t(wsp)
    g["l1_wval_t"] = _t(inputs["L1_attn_W_val"])
    g["l1_wout_t"] = _t(inputs["L1_attn_W_out"])
    g["l1_g1_col"] = _f32(inputs["L1_ffn_ln1_g"]).reshape(D, 1)
    g["l1_b1ln_col"] = _f32(inputs["L1_ffn_ln1_b"]).reshape(D, 1)
    g["l1_w1_t"] = _t(inputs["L1_ffn_w1"])
    g["l1_b1_col"] = _f32(inputs["L1_ffn_b1"]).reshape(4096, 1)
    g["l1_g2_col"] = _f32(inputs["L1_ffn_ln2_g"]).reshape(4096, 1)
    g["l1_b2ln_col"] = _f32(inputs["L1_ffn_ln2_b"]).reshape(4096, 1)
    g["l1_w2_t"] = _t(inputs["L1_ffn_w2"])
    g["l1_b2_col"] = _f32(inputs["L1_ffn_b2"]).reshape(D, 1)
    perm = np.zeros(D, dtype=np.int64)
    for h in range(16):
        bb = 64 * h
        perm[bb:bb + 32] = bb + np.arange(0, 64, 2)
        perm[bb + 32:bb + 64] = bb + np.arange(1, 64, 2)
    g["l2_wq_t"] = _t(_f32(inputs["L2_attn_Wq"])[perm, :])
    g["l2_wk_t"] = _t(_f32(inputs["L2_attn_Wk"])[perm, :])
    g["l2_wv_t"] = _t(inputs["L2_attn_Wv"])
    g["l2_wo_t"] = _t(inputs["L2_attn_Wo"])
    g["l2_sqk_cc"] = np.ascontiguousarray(
        _f32(inputs["L2_attn_s_qk"]).reshape(D)[perm].reshape(8, 128).T)
    g["l2_wgate"] = _f32(inputs["L2_ffn_w_gate"])
    g["l2_wup"] = _f32(inputs["L2_ffn_w_up"])
    g["l2_wdown"] = _f32(inputs["L2_ffn_w_down"])
    g["l3_wq"] = _f32(inputs["L3_attn_Wq"])
    g["l3_wk"] = _f32(inputs["L3_attn_Wk"])
    g["l3_wv"] = _f32(inputs["L3_attn_Wv"])
    g["l3_bq_row"] = _f32(inputs["L3_attn_bq"]).reshape(1, D)
    g["l3_bk_row"] = _f32(inputs["L3_attn_bk"]).reshape(1, D)
    g["l3_bv_row"] = _f32(inputs["L3_attn_bv"]).reshape(1, D)
    g["l3_wf"] = _f32(inputs["L3_ffn_weight"])
    g["l3_bf_col"] = np.ascontiguousarray(_f32(inputs["L3_ffn_bias"]).reshape(8, 128).T)
    g["l3_dw"] = _f32(inputs["L3_ffn_delta_w"])
    g["l3_dact"] = _f32(inputs["L3_ffn_delta_act"]).reshape(1, 1)
    g["c_ident"] = np.eye(128, dtype=BF)
    g["c_identf"] = np.eye(128, dtype=np.float32)
    pl = np.arange(128, dtype=np.float64) - 63.5
    p2 = pl * pl
    p2h = p2.astype(BF).astype(np.float64)
    p2l = p2 - p2h
    basis = np.zeros((8, 128, 128), dtype=np.float64)
    for ja in range(8):
        for r in range(3):
            basis[ja, 16 * ja + 0 + r, :] = p2h
            basis[ja, 16 * ja + 3 + r, :] = p2l
            basis[ja, 16 * ja + 6 + r, :] = pl
            basis[ja, 16 * ja + 9 + r, :] = 1.0
    g["c_l1basis"] = basis.reshape(8 * 128, 128).astype(BF)
    g["c_pos_row"] = np.arange(T, dtype=np.float32).reshape(1, T)
    cj = np.zeros((128, 256), dtype=np.float32)
    for j in range(16):
        cj[:, 16 * j:16 * j + 16] = 128.0 * j + 63.5
    g["c_l1cj"] = cj
    g["l1_b1_pc"] = np.ascontiguousarray(_f32(inputs["L1_ffn_b1"]).reshape(32, 128).T)
    g["l1_b1ln_pc"] = np.ascontiguousarray(_f32(inputs["L1_ffn_ln1_b"]).reshape(8, 128).T)
    g["l1_g2_pc"] = np.ascontiguousarray(_f32(inputs["L1_ffn_ln2_g"]).reshape(32, 128).T)
    g["l1_b2ln_pc"] = np.ascontiguousarray(_f32(inputs["L1_ffn_ln2_b"]).reshape(32, 128).T)
    g["l1_b2_pc"] = np.ascontiguousarray(_f32(inputs["L1_ffn_b2"]).reshape(8, 128).T)
    g["c_rope_e"] = (-(np.arange(32, dtype=np.float32) * 2.0 / 64.0)
                     * np.float32(np.log(np.float32(10000.0)))).reshape(32, 1)
    ind4 = np.zeros((128, 4), dtype=BF)
    exp4 = np.zeros((4, 128), dtype=np.float32)
    for h in range(4):
        ind4[32 * h:32 * h + 32, h] = 1
        exp4[h, 32 * h:32 * h + 32] = 1
    g["c_ind4"] = ind4
    g["c_exp4"] = exp4
    return g


class Bld:
    def __init__(self, nc, es, g):
        self.nc = nc
        self.P = Prog(nc, es)
        sb = lambda n, s, d: es.enter_context(nc.sbuf_tensor(n, s, d))
        ps = lambda n, s, d=F32: es.enter_context(nc.psum_tensor(n, s, d))
        self.din = {}
        for k, v in g.items():
            dt = BF16 if v.dtype == BF else F32
            self.din[k] = nc.dram_tensor(k, list(v.shape), dt, kind="ExternalInput").ap()
        self.x_in = nc.dram_tensor("x_in", [D, T], F32, kind="ExternalInput").ap()
        self.y_out = nc.dram_tensor("y_out", [D, T], F32, kind="ExternalOutput").ap()
        dr = lambda n, s, d: nc.dram_tensor(n, s, d).ap()
        self.scr_row = dr("scr_row", [1, T], F32)
        self.scr_row2 = dr("scr_row2", [1, T], F32)
        self.scr_rowZ = dr("scr_rowZ", [1, T], F32)
        self.scr_col = dr("scr_col", [1, 128], F32)
        self.w0q_d = dr("w0q_d", [D, D], BF16)
        self.w1q_d = dr("w1q_d", [D, 4096], BF16)
        self.w2q_d = dr("w2q_d", [4096, D], BF16)
        self.wgq_d = dr("wgq_d", [3072, D], BF16)
        self.wuq_d = dr("wuq_d", [3072, D], BF16)
        self.wdq_d = dr("wdq_d", [D, 3072], BF16)
        self.wq3_d = dr("wq3_d", [D, D], BF16)
        self.wk3_d = dr("wk3_d", [D, D], BF16)
        self.wv3_d = dr("wv3_d", [D, D], BF16)
        self.wf3_d = dr("wf3_d", [D, D], BF16)
        self.up_sp = dr("up_sp", [3072, T], BF16)
        self.gate_sp = dr("gate_sp", [3072, T], BF16)
        self.t_sp = dr("t_sp", [3072, T], BF16)
        self.h_sp = dr("h_sp", [4096, T], BF16)
        self.cc_in = dr("cc_in", [1, 16], F32)
        self.cc_out = dr("cc_out", [1, 16], F32)
        self.scr32 = dr("scr32", [32, 2048], F32)
        self.scr64 = dr("scr64", [64, 2048], F32)
        self.ctx_sp = dr("ctx_sp", [D, 2048], BF16)
        self.x = sb("x", [128, NCH * T], F32)
        self.xn16 = sb("xn16", [128, NCH * T], BF16)
        self.b1 = sb("b1", [128, NCH * T], BF16)
        self.b2 = sb("b2", [128, NCH * T], BF16)
        self.wbuf = sb("wbuf", [128, 4096], BF16)
        self.tfA = sb("tfA", [128, 2048], F32)
        self.tfB = sb("tfB", [128, 2048], F32)
        self.bcA = sb("bcA", [128, 2048], F32)
        self.rows = sb("rows", [128, 2048], F32)
        self.brow = sb("brow", [1, 512], BF16)
        self.small = sb("small", [128, 512], F32)
        self.small16 = sb("small16", [128, 1024], BF16)
        self.cols = sb("cols", [128, 80], F32)
        self.ccol = sb("ccol", [128, 48], F32)
        self.ident = sb("ident", [128, 128], BF16)
        self.identf = sb("identf", [128, 128], F32)
        self.cind4 = sb("cind4", [128, 4], BF16)
        self.cexp4 = sb("cexp4", [4, 128], F32)
        self.onescol = sb("onescol", [128, 1], BF16)
        self.onescolf = sb("onescolf", [128, 1], F32)
        self.onesrow = sb("onesrow", [1, 512], BF16)
        self.pb = [ps(f"pb{i}", [128, 512]) for i in range(6)]
        self.ptr = ps("ptr", [128, 1024], BF16)
        self._rr = 0

    def bank(self):
        i = self._rr % 4
        self._rr += 1
        return i

    def dma(self, dst, src, queue="qsync", reads=(), writes=()):
        eng = "sync" if queue in ("qsync", "qout") else "gpsimd"
        self.P.emit(eng, lambda e: e.dma_start(out=dst, in_=src),
                    reads=reads, writes=writes, queue=queue)

    def mm(self, out, lhsT, rhs, start, stop, reads=(), writes=()):
        self.P.emit("tensor",
                    lambda e: e.matmul(out, lhsT, rhs, start=start, stop=stop),
                    reads=reads, writes=writes)

    def act(self, out, in_, func, bias=0.0, scale=1.0, accum=None,
            reads=(), writes=(), eng="scalar"):
        self.P.emit(eng, lambda e: e.activation(out, in_, func, bias=bias,
                                                scale=scale, accum_out=accum),
                    reads=reads, writes=writes)

    def vcopy(self, dst, src, reads=(), writes=(), eng="vector"):
        self.P.emit(eng, lambda e: e.tensor_copy(dst, src), reads=reads, writes=writes)

    def ts(self, dst, src, s1, s2, op0, op1=None, reads=(), writes=(), eng="vector"):
        if op1 is None:
            self.P.emit(eng, lambda e: e.tensor_scalar(dst, src, s1, None, op0),
                        reads=reads, writes=writes)
        else:
            self.P.emit(eng, lambda e: e.tensor_scalar(dst, src, s1, s2, op0, op1),
                        reads=reads, writes=writes)

    def tt(self, dst, a, bb, op, reads=(), writes=(), eng="vector"):
        self.P.emit(eng, lambda e: e.tensor_tensor(dst, a, bb, op),
                    reads=reads, writes=writes)

    def stt(self, dst, in0, scalar, in1, op0, op1, reads=(), writes=(), eng="vector"):
        self.P.emit(eng, lambda e: e.scalar_tensor_tensor(dst, in0, scalar, in1, op0, op1),
                    reads=reads, writes=writes)

    def bcast_row(self, src_row, dst, n, key, outkey, scr=None, parts=128):
        scr = self.scr_row if scr is None else scr
        skey = ("scrrow",) if scr is self.scr_row else (
            ("scrrow2",) if scr is self.scr_row2 else ("scrrowZ",))
        self.dma(scr[:, 0:n], src_row, queue="qout", reads=[key], writes=[skey])
        self.dma(dst, scr[:, 0:n].broadcast_to([parts, n]), queue="qsync",
                 reads=[skey], writes=[outkey])

    def bcast_col(self, src_col_128x1, dst_col, key, outkey):
        skey = ("scrcol",)
        self.dma(self.scr_col[:, 0:1], src_col_128x1, queue="qout",
                 reads=[key], writes=[skey])
        self.dma(dst_col, self.scr_col[:, 0:1].broadcast_to([128, 1]), queue="qsync",
                 reads=[skey], writes=[outkey])

    def load_wt_block(self, wt_ap, oc, ocols, dkey, cast=True, kchunks=NCH):
        q = "qgp" if cast else "qsync"
        w = wt_ap.tensor.shape[1]
        srcap = bass.AP(wt_ap.tensor, oc, [[w, 128], [128 * w, kchunks], [1, ocols]])
        dstap = bass.AP(self.wbuf[:].tensor, 0, [[4096, 128], [512, kchunks], [1, ocols]])
        self.dma(dstap, srcap, queue=q,
                 writes=[("wbuf", d) for d in range(kchunks)])

    def load_bias_row(self, bias_ap, oc, ocols, key):
        self.dma(self.brow[:, 0:ocols], bias_ap[:, oc:oc + ocols], queue="qgp",
                 writes=[key])

    def start_amax_collective(self, slot, colkey, colidx, n=1):
        P = self.P
        self.vcopy(self.small16[0:128, 0:n], self.cols[:, colidx:colidx + n],
                   reads=[colkey], writes=[("sm16",)])
        P.emit("tensor", lambda e: e.transpose(self.ptr[0:n, 0:128],
                                               self.small16[:, 0:n], self.ident[:]),
               reads=[("sm16",), ("ident",)], writes=[("ptr",)])
        P.emit("vector", lambda e: e.reduce_max(self.small[0:n, 30:31], self.ptr[0:n, 0:128],
                                                axis=AX.X),
               reads=[("ptr",)], writes=[("ccl", slot)])
        dst = self.cc_in[:, slot:slot + n]
        if n > 1:
            dst = dst.rearrange("one (n o2) -> (one n) o2", o2=1)
        self.dma(dst, self.small[0:n, 30:31],
                 queue="qout", reads=[("ccl", slot)], writes=[("ccin",)])
        P.emit("gpsimd", lambda e: e.collective_compute(
            "AllReduce", ALU.max, replica_groups=[list(range(CC_CORES))],
            ins=[self.cc_in[:, :]], outs=[self.cc_out[:, :]]),
            reads=[("ccin",)], writes=[("ccout",)], queue="cc", step=1)

    def finish_amax_scale(self, slot, colS, colR, qmax=32767.0, lo=1e-12):
        self.dma(self.small[0:1, 28:29], self.cc_out[:, slot:slot + 1],
                 queue="qsync", reads=[("ccout",)], writes=[("ccs", slot)])
        self.ts(self.small[0:1, 28:29], self.small[0:1, 28:29], 1.0 / qmax, lo,
                ALU.mult, ALU.max, reads=[("ccs", slot)], writes=[("ccs", slot)])
        self.bcast_col(self.small[0:1, 28:29], self.cols[:, colS:colS + 1],
                       ("ccs", slot), ("sc", colS))
        self.P.emit("vector", lambda e: e.reciprocal(self.cols[:, colR:colR + 1],
                                                     self.cols[:, colS:colS + 1]),
                    reads=[("sc", colS)], writes=[("sc", colR)])

    def ln_stats(self, tag):
        rows, pb = self.rows, self.pb
        for tb in range(NTB):
            for d in range(NCH):
                c = slice(d * T + 512 * tb, d * T + 512 * tb + 512)
                self.vcopy(self.small16[:, 0:512], self.x[:, c],
                           reads=[("x", d)], writes=[("sm16",)])
                self.act(self.small16[:, 512:1024], self.small16[:, 0:512], AF.Square,
                         reads=[("sm16",)], writes=[("sm16",)])
                self.mm(pb[4][0:1, :], self.onescol[:], self.small16[:, 0:512],
                        start=(d == 0), stop=(d == NCH - 1),
                        reads=[("sm16",), ("onescol",)], writes=[("ps", 4)])
                self.mm(pb[5][0:1, :], self.onescol[:], self.small16[:, 512:1024],
                        start=(d == 0), stop=(d == NCH - 1),
                        reads=[("sm16",), ("onescol",)], writes=[("ps", 5)])
            self.vcopy(rows[0:1, 512 * tb:512 * tb + 512], pb[4][0:1, :],
                       reads=[("ps", 4)], writes=[("rows", 0)])
            self.vcopy(self.tfA[0:1, 512 * tb:512 * tb + 512], pb[5][0:1, :],
                       reads=[("ps", 5)], writes=[("tfA",)])
        self.ts(rows[0:1, 0:T], rows[0:1, 0:T], 1.0 / D, None, ALU.mult,
                reads=[("rows", 0)], writes=[("rows", 0)])
        self.tt(self.tfB[0:1, 0:T], rows[0:1, 0:T], rows[0:1, 0:T], ALU.mult,
                reads=[("rows", 0)], writes=[("tfB",)])
        self.stt(self.tfA[0:1, 0:T], self.tfA[0:1, 0:T], 1.0 / D, self.tfB[0:1, 0:T],
                 ALU.mult, ALU.subtract,
                 reads=[("tfA",), ("tfB",)], writes=[("tfA",)])
        self.act(self.tfA[0:1, 0:T], self.tfA[0:1, 0:T], AF.Ln, bias=self.cols[0:1, 36:37],
                 reads=[("tfA",), ("epscol",)], writes=[("tfA",)])
        self.act(self.tfA[0:1, 0:T], self.tfA[0:1, 0:T], AF.Exp, scale=-0.5,
                 reads=[("tfA",)], writes=[("tfA",)])
        self.stt(self.tfB[0:1, 0:T], rows[0:1, 0:T], -1.0, self.tfA[0:1, 0:T],
                 ALU.mult, ALU.mult,
                 reads=[("rows", 0), ("tfA",)], writes=[("tfB",)])
        self.bcast_row(self.tfA[0:1, 0:T], self.bcA[:, 0:T], T, ("tfA",), ("bcA",))
        self.bcast_row(self.tfB[0:1, 0:T], self.tfB[:, 0:T], T, ("tfB",), ("tfB",),
                       scr=self.scr_row2)

    def make_xn(self):
        for d in range(NCH):
            c = slice(d * T, (d + 1) * T)
            self.tt(self.tfA[:, 0:T], self.x[:, c], self.bcA[:, 0:T], ALU.mult,
                    reads=[("x", d), ("bcA",)], writes=[("tfA",)])
            self.tt(self.xn16[:, c], self.tfA[:, 0:T], self.tfB[:, 0:T], ALU.add,
                    reads=[("tfA",), ("tfB",)], writes=[("xn16", d)])

    def xn_f32_tile(self, d, dst, dkey):
        c = slice(d * T, (d + 1) * T)
        self.tt(self.tfA[:, 0:T], self.x[:, c], self.bcA[:, 0:T], ALU.mult,
                reads=[("x", d), ("bcA",)], writes=[("tfA",)])
        self.tt(dst, self.tfA[:, 0:T], self.tfB[:, 0:T], ALU.add,
                reads=[("tfA",), ("tfB",)], writes=[dkey])

    def residual_sg(self, d, tb, g_tile, g_key):
        c = slice(d * T + 512 * tb, d * T + 512 * tb + 512)
        s_slice = self.b1[:, c]
        self.tt(self.tfA[:, 0:512], s_slice, g_tile, ALU.mult,
                reads=[("s16", d), g_key], writes=[("tfA",)])
        self.tt(self.x[:, c], self.x[:, c], self.tfA[:, 0:512], ALU.add,
                reads=[("x", d), ("tfA",)], writes=[("x", d)])


def emit_L0(b: Bld):
    P, din = b.P, b.din
    pb, rows = b.pb, b.rows
    b.ln_stats("l0")
    b.make_xn()

    def xn_lhs(d, tt):
        return b.xn16[:, d * T + 128 * tt: d * T + 128 * tt + 128]

    for half, name, tgt in ((0, "k", b.b1), (1, "v", b.b2)):
        for blk in range(2):
            oc_w = 1024 + half * 1024 + blk * 512
            b.load_wt_block(din["l0_wqkv_t"], oc_w, 512, ("w0kv", half, blk))
            b.load_bias_row(din["l0_bqkv_row"], oc_w, 512, ("brow",))
            for tt in range(TT):
                bi = b.bank()
                pw = ("ps", bi)
                for d in range(NCH):
                    b.mm(pb[bi][:], xn_lhs(d, tt), b.wbuf[:, 512 * d:512 * d + 512],
                         start=(d == 0), stop=False,
                         reads=[("wbuf", d), ("xn16", d)], writes=[pw])
                b.mm(pb[bi][:], b.onesrow[0:1, 0:128], b.brow[0:1, 0:512],
                     start=False, stop=True,
                     reads=[("onesrow",), ("brow",)], writes=[pw])
                dst = tgt[:, tt * 1024 + blk * 512: tt * 1024 + blk * 512 + 512]
                b.vcopy(dst, pb[bi][:], reads=[pw], writes=[(name, tt, blk)])

    for g in range(8):
        bi = 4 + g // 4
        for tt in range(TT):
            kc = tt * 1024 + 128 * g
            b.mm(pb[bi][:, 128 * (g % 4):128 * (g % 4) + 128],
                 b.b1[:, kc:kc + 128], b.b2[:, kc:kc + 128],
                 start=(tt == 0), stop=(tt == TT - 1),
                 reads=[("k", tt, 0), ("k", tt, 1), ("v", tt, 0), ("v", tt, 1)],
                 writes=[("Mps", g), ("ps", bi)])
    for blk in range(2):
        b.load_wt_block(din["l0_wqkv_t"], blk * 512, 512, ("w0q", blk))
        b.load_bias_row(din["l0_bqkv_row"], blk * 512, 512, ("brow",))
        for o128 in range(4):
            g = blk * 4 + (o128 * 128) // 128 // 1
            g = blk * 4 + o128
            for tb in range(NTB):
                bi = b.bank()
                pw = ("ps", bi)
                for d in range(NCH):
                    b.mm(pb[bi][:], b.wbuf[:, 512 * d + 128 * o128: 512 * d + 128 * o128 + 128],
                         b.xn16[:, d * T + 512 * tb: d * T + 512 * tb + 512],
                         start=(d == 0), stop=False,
                         reads=[("wbuf", d), ("xn16", d)], writes=[pw])
                b.mm(pb[bi][:], b.brow[0:1, 128 * o128:128 * o128 + 128],
                     b.onesrow[0:1, 0:512], start=False, stop=True,
                     reads=[("brow",), ("onesrow",)], writes=[pw])
                dst = b.b2[:, g * T + 512 * tb: g * T + 512 * tb + 512]
                b.vcopy(dst, pb[bi][:], reads=[pw], writes=[("qfm", g, tb)])

    for g in range(8):
        b.P.emit("vector", lambda e: e.memset(b.small16[:, 0:128], 0.0),
                 reads=(), writes=[("sm16",)])
        for i in range(4):
            src = pb[4 + g // 4][32 * i:32 * i + 32,
                                 128 * (g % 4) + 32 * i:128 * (g % 4) + 32 * i + 32]
            b.vcopy(b.small16[32 * i:32 * i + 32, 32 * i:32 * i + 32], src,
                    reads=[("Mps", g)], writes=[("sm16",)])
        for tb in range(NTB):
            bi = b.bank()
            pw = ("ps", bi)
            b.mm(pb[bi][:], b.small16[:, 0:128],
                 b.b2[:, g * T + 512 * tb: g * T + 512 * tb + 512],
                 start=True, stop=True,
                 reads=[("sm16",), ("qfm", g, tb)], writes=[pw])
            dst = b.b2[:, g * T + 512 * tb: g * T + 512 * tb + 512]
            b.vcopy(dst, pb[bi][:], reads=[pw], writes=[("ctx", g, tb)])

    for g in range(8):
        for tb in range(NTB):
            c = slice(g * T + 512 * tb, g * T + 512 * tb + 512)
            bi = b.bank()
            pw = ("ps", bi)
            b.mm(pb[bi][0:4, :], b.cind4[:], b.b2[:, c], start=True, stop=True,
                 reads=[("ctx", g, tb), ("ind4",)], writes=[pw])
            b.P.emit("vector",
                     lambda e, g=g, tb=tb, bi=bi: e.reduce_sum(
                         b.small[0:4, 16 * g + tb:16 * g + tb + 1], pb[bi][0:4, :], axis=AX.X),
                     reads=[pw], writes=[("gnS", g, tb)])
            b.act(b.small16[:, 128:640], b.b2[:, c], AF.Square,
                  reads=[("ctx", g, tb)], writes=[("sm16",)])
            bi2 = b.bank()
            pw2 = ("ps", bi2)
            b.mm(pb[bi2][0:4, :], b.cind4[:], b.small16[:, 128:640],
                 start=True, stop=True, reads=[("sm16",), ("ind4",)], writes=[pw2])
            b.P.emit("vector",
                     lambda e, g=g, tb=tb, bi2=bi2: e.reduce_sum(
                         b.small[0:4, 128 + 16 * g + tb:128 + 16 * g + tb + 1],
                         pb[bi2][0:4, :], axis=AX.X),
                     reads=[pw2], writes=[("gnQ", g, tb)])
    b.P.emit("vector",
             lambda e: e.reduce_sum(b.small[0:4, 256:264].unsqueeze(2),
                                    bass.AP(b.small[:].tensor, 0,
                                            [[512, 4], [16, 8], [1, NTB]]), axis=AX.X),
             reads=[("gnS", g2, t2) for g2 in range(8) for t2 in range(NTB)],
             writes=[("gnSs",)])
    b.P.emit("vector",
             lambda e: e.reduce_sum(b.small[0:4, 264:272].unsqueeze(2),
                                    bass.AP(b.small[:].tensor, 128,
                                            [[512, 4], [16, 8], [1, NTB]]), axis=AX.X),
             reads=[("gnQ", g2, t2) for g2 in range(8) for t2 in range(NTB)],
             writes=[("gnQs",)])
    NG = 32.0 * T
    b.ts(b.small[0:4, 256:264], b.small[0:4, 256:264], 1.0 / NG, None, ALU.mult,
         reads=[("gnSs",)], writes=[("gnSs",)])
    b.tt(b.small[0:4, 272:280], b.small[0:4, 256:264], b.small[0:4, 256:264], ALU.mult,
         reads=[("gnSs",)], writes=[("gnM2",)])
    b.stt(b.small[0:4, 264:272], b.small[0:4, 264:272], 1.0 / NG,
          b.small[0:4, 272:280], ALU.mult, ALU.subtract,
          reads=[("gnQs",), ("gnM2",)], writes=[("gnQs",)])
    b.act(b.small[0:4, 264:272], b.small[0:4, 264:272], AF.Ln, bias=b.cols[0:4, 36:37],
          reads=[("gnQs",), ("epscol",)], writes=[("gnQs",)])
    b.act(b.small[0:4, 264:272], b.small[0:4, 264:272], AF.Exp, scale=-0.5,
          reads=[("gnQs",)], writes=[("gnQs",)])
    mps_keys = [("Mps", gg) for gg in range(8)]
    for g in range(8):
        b.mm(pb[4][:, 0:1], b.cexp4[:], b.small[0:4, 264 + g:264 + g + 1],
             start=True, stop=True, reads=[("gnQs",), ("exp4",)],
             writes=[("ps", 4)] + (mps_keys if g == 0 else []))
        b.vcopy(b.cols[:, g:g + 1], pb[4][:, 0:1], reads=[("ps", 4)], writes=[("gcr", g)])
        b.mm(pb[4][:, 0:1], b.cexp4[:], b.small[0:4, 256 + g:256 + g + 1],
             start=True, stop=True, reads=[("gnSs",), ("exp4",)], writes=[("ps", 4)])
        b.vcopy(b.cols[:, 8 + g:8 + g + 1], pb[4][:, 0:1], reads=[("ps", 4)], writes=[("gcm", g)])
    for g in range(8):
        b.tt(b.cols[:, 16 + g:16 + g + 1], b.cols[:, g:g + 1],
             b.ccol[:, 0 + g:0 + g + 1], ALU.mult,
             reads=[("gcr", g), ("gnw",)], writes=[("gal", g)])
        b.stt(b.cols[:, 24 + g:24 + g + 1], b.cols[:, 8 + g:8 + g + 1], -1.0,
              b.cols[:, 16 + g:16 + g + 1], ALU.mult, ALU.mult,
              reads=[("gcm", g), ("gal", g)], writes=[("gbe", g)])
        b.tt(b.cols[:, 24 + g:24 + g + 1], b.cols[:, 24 + g:24 + g + 1],
             b.ccol[:, 8 + g:8 + g + 1], ALU.add,
             reads=[("gbe", g), ("gnb",)], writes=[("gbe", g)])
    for g in range(8):
        for tb in range(NTB):
            c = slice(g * T + 512 * tb, g * T + 512 * tb + 512)
            b.ts(b.b2[:, c], b.b2[:, c], b.cols[:, 16 + g:16 + g + 1],
                 b.cols[:, 24 + g:24 + g + 1], ALU.mult, ALU.add,
                 reads=[("ctx", g, tb), ("gal", g), ("gbe", g)],
                 writes=[("ctx", g, tb)])

    for blk in range(2):
        b.load_wt_block(din["l0_wout_t"], blk * 512, 512, ("w0o", blk))
        b.load_bias_row(din["l0_bout_row"], blk * 512, 512, ("brow",))
        for o128 in range(4):
            dch = blk * 4 + o128
            for tb in range(NTB):
                bi = b.bank()
                pw = ("ps", bi)
                for d in range(NCH):
                    b.mm(pb[bi][:], b.wbuf[:, 512 * d + 128 * o128:512 * d + 128 * o128 + 128],
                         b.b2[:, d * T + 512 * tb: d * T + 512 * tb + 512],
                         start=(d == 0), stop=False,
                         reads=[("wbuf", d), ("ctx", d, tb)], writes=[pw])
                b.mm(pb[bi][:], b.brow[0:1, 128 * o128:128 * o128 + 128],
                     b.onesrow[0:1, 0:512], start=False, stop=True,
                     reads=[("brow",), ("onesrow",)], writes=[pw])
                dst = b.b1[:, dch * T + 512 * tb: dch * T + 512 * tb + 512]
                b.vcopy(dst, pb[bi][:], reads=[pw], writes=[("s16", dch)])

    for blk in range(2):
        b.load_wt_block(b.w0q_d, blk * 512, 512, ("w0f", blk), cast=False)
        for o128 in range(4):
            dch = blk * 4 + o128
            for tb in range(NTB):
                bi = b.bank()
                pw = ("ps", bi)
                for d in range(NCH):
                    b.mm(pb[bi][:], b.wbuf[:, 512 * d + 128 * o128:512 * d + 128 * o128 + 128],
                         b.xn16[:, d * T + 512 * tb: d * T + 512 * tb + 512],
                         start=(d == 0), stop=(d == NCH - 1),
                         reads=[("wbuf", d), ("xn16", d)], writes=[pw])
                b.act(b.small16[:, 0:512], pb[bi][:], AF.Silu,
                      bias=b.ccol[:, 16 + dch:16 + dch + 1],
                      scale=b.cols[:, 32:33],
                      reads=[pw, ("bf0",), ("alpha0",)], writes=[("sm16",)])
                b.residual_sg(dch, tb, b.small16[:, 0:512], ("sm16",))


def emit_quant_L0(b: Bld):
    P, din, pb = b.P, b.din, b.pb
    for d in range(NCH):
        b.dma(b.tfA[:, 0:1024], din["l0_wf_t"][128 * d:128 * d + 128, :],
              queue="qsync", writes=[("tfA",)])
        b.act(b.small16[:, 0:1024], b.tfA[:, 0:1024], AF.Abs,
              reads=[("tfA",)], writes=[("sm16",)])
        for half in range(2):
            b.mm(pb[5][0:1, :], b.onescol[:], b.small16[:, 512 * half:512 * half + 512],
                 start=(d == 0 and half == 0), stop=(d == NCH - 1 and half == 1),
                 reads=[("sm16",), ("onescol",)], writes=[("ps", 5)])
    b.P.emit("vector", lambda e: e.reduce_sum(b.small[0:1, 0:1], pb[5][0:1, :], axis=AX.X),
             reads=[("ps", 5)], writes=[("qdsum",)])
    b.ts(b.small[0:1, 0:1], b.small[0:1, 0:1], 0.7 / (D * D), None, ALU.mult,
         reads=[("qdsum",)], writes=[("qdsum",)])
    b.ts(b.small[0:1, 1:2], b.small[0:1, 0:1], -1.0, None, ALU.mult,
         reads=[("qdsum",)], writes=[("qdsumn",)])
    b.bcast_col(b.small[0:1, 0:1], b.cols[:, 34:35], ("qdsum",), ("dcol",))
    b.bcast_col(b.small[0:1, 1:2], b.cols[:, 35:36], ("qdsumn",), ("ndcol",))
    for d in range(NCH):
        b.dma(b.tfA[:, 0:1024], din["l0_wf_t"][128 * d:128 * d + 128, :],
              queue="qsync", writes=[("tfA",)])
        b.ts(b.tfB[:, 0:1024], b.tfA[:, 0:1024], b.cols[:, 34:35], None, ALU.is_gt,
             reads=[("tfA",), ("dcol",)], writes=[("tfB",)])
        b.ts(b.tfB[:, 1024:2048], b.tfA[:, 0:1024], b.cols[:, 35:36], None, ALU.is_lt,
             reads=[("tfA",), ("ndcol",), ("tfB",)], writes=[("tfB",)])
        b.tt(b.small16[:, 0:1024], b.tfB[:, 0:1024], b.tfB[:, 1024:2048], ALU.subtract,
             reads=[("tfB",)], writes=[("sm16",)])
        b.dma(b.w0q_d[128 * d:128 * d + 128, :], b.small16[:, 0:1024],
              queue="qout", reads=[("sm16",)], writes=[("w0qd", d)])


def build(g, layers):
    nc = bass.Bass("TRN2", target_bir_lowering=False)
    es = contextlib.ExitStack()
    with es:
        b = Bld(nc, es, g)
        P = b.P
        with nc.Block() as block:
            for d in range(NCH):
                b.dma(b.x[:, d * T:(d + 1) * T], b.x_in[128 * d:128 * d + 128, :],
                      queue="qsync", writes=[("x", d)])
            b.dma(b.ident[:], b.din["c_ident"], queue="qsync", writes=[("ident",)])
            b.dma(b.identf[:], b.din["c_identf"], queue="qsync", writes=[("identf",)])
            P.emit("vector", lambda e: e.memset(b.onescol[:], 1.0), writes=[("onescol",)])
            P.emit("vector", lambda e: e.memset(b.onescolf[:], 1.0), writes=[("onescol",)])
            P.emit("vector", lambda e: e.memset(b.cols[:, 36:37], LN_EPS), writes=[("epscol",)])
            P.emit("vector", lambda e: e.memset(b.onesrow[:], 1.0), writes=[("onesrow",)])
            b.dma(b.cind4[:], b.din["c_ind4"], queue="qsync", writes=[("ind4",)])
            b.dma(b.cexp4[:], b.din["c_exp4"], queue="qsync", writes=[("exp4",)])
            b.dma(b.ccol[:, 0:8], b.din["l0_gnw_col"], queue="qsync", writes=[("gnw",)])
            b.dma(b.ccol[:, 8:16], b.din["l0_gnb_col"], queue="qsync", writes=[("gnb",)])
            b.dma(b.ccol[:, 16:24], b.din["l0_bf_col"], queue="qsync", writes=[("bf0",)])
            b.dma(b.cols[:, 32:33], b.din["l0_alpha"].broadcast_to([128, 1]),
                  queue="qsync", writes=[("alpha0",)])

            b.dma(b.ccol[:, 24:32], b.din["l3_bf_col"], queue="qsync", writes=[("bf3",)])
            P.emit("vector", lambda e: e.memset(b.small[0:1, 0:16], 0.0), writes=[("cc0",)])
            b.dma(b.cc_in[:, :], b.small[0:1, 0:16], queue="qout",
                  reads=[("cc0",)], writes=[("ccin",)])
            if 0 in layers:
                emit_quant_L0(b)
            if 1 in layers:
                emit_quant_L1(b)
            if 2 in layers:
                emit_quant_L2(b)
            if 3 in layers:
                emit_quant_L3(b)
            b.P.full_fence()
            if 0 in layers:
                emit_L0(b)
            if 1 in layers:
                b.P.full_fence()
                emit_L1(b)
            if 2 in layers:
                b.P.full_fence()
                emit_L2(b)
            if 3 in layers:
                b.P.full_fence()
                emit_L3(b)
            b.P.full_fence()

            for d in range(NCH):
                b.dma(b.y_out[128 * d:128 * d + 128, :], b.x[:, d * T:(d + 1) * T],
                      queue="qout", reads=[("x", d)], writes=[("yout", d)])
            P.wait_all("sync")
            P.finalize(block)
    return nc


def kernel(**inputs):
    g = host_prep(inputs)
    x_full = _f32(inputs["L0_attn_x"])[:, :T, :]
    layers = [int(ONLY)] if ONLY else list(range(N_LAYERS))
    nc = build(g, layers)
    in_maps = []
    for core in range(8):
        m = dict(g)
        m["x_in"] = np.ascontiguousarray(x_full[core].T)
        in_maps.append(m)
    res = run_bass_kernel_spmd(nc, in_maps, core_ids=list(range(8)))
    out = np.stack([np.ascontiguousarray(r["y_out"].T) for r in res.results])
    return out


def emit_mxfp4_half(b: Bld, w_ap, wkey, out16_ap, outkey, delta_col_ap, dkeys,
                    part_blocks, dcol_n):
    P = b.P
    n = 1024
    nb = 32
    b.act(b.tfB[:, 0:n], w_ap, AF.Abs, reads=[wkey], writes=[("tfB",)])
    if part_blocks:
        P.emit("vector", lambda e: e.transpose(b.tfB[:, 1024:2048], b.tfB[:, 0:1024]),
               reads=[("tfB",)], writes=[("tfB",)])
        bm_src = b.tfB[:, 1024:2048]
    else:
        bm_src = b.tfB[:, 0:n]
    P.emit("vector", lambda e, s=bm_src: e.reduce_max(
        b.small[:, 64:64 + nb].unsqueeze(2),
        s.rearrange("p (bk w) -> p bk w", w=32), axis=AX.X),
        reads=[("tfB",)], writes=[("mxm",)])
    b.ts(b.small[:, 96:96 + nb].bitcast(I32),
         b.small[:, 64:64 + nb].bitcast(I32), 23, None,
         ALU.logical_shift_right, reads=[("mxm",)], writes=[("mxe",)])
    b.ts(b.small[:, 64:64 + nb], b.small[:, 96:96 + nb].bitcast(I32),
         -129.0, None, ALU.add, reads=[("mxe",)], writes=[("mxsl",)])
    if part_blocks:
        b.ts(b.small[:, 64:64 + nb], b.small[:, 64:64 + nb], delta_col_ap, None,
             ALU.add, reads=[("mxsl",)] + list(dkeys), writes=[("mxsl",)])
    else:
        b.tt(b.small[:, 64:64 + nb], b.small[:, 64:64 + nb], delta_col_ap, ALU.add,
             reads=[("mxsl",)] + list(dkeys), writes=[("mxsl",)])
    LN2 = float(np.log(2.0))
    b.act(b.small[:, 96:96 + nb], b.small[:, 64:64 + nb], AF.Exp, scale=LN2,
          reads=[("mxsl",)], writes=[("mxsc",)])
    b.act(b.small[:, 64:64 + nb], b.small[:, 64:64 + nb], AF.Exp, scale=-LN2,
          reads=[("mxsl",), ("mxsc",)], writes=[("mxsr",)])
    def expand(colrange0, dstslice, rkey, wkey2, dregion):
        rep = bass.AP(b.small[:].tensor, colrange0, [[512, 128], [1, nb], [0, 32]])
        if part_blocks:
            P.emit("vector", lambda e: e.tensor_copy(
                b.bcA[:, 0:n].rearrange("p (bk w) -> p bk w", w=32), rep),
                reads=[rkey], writes=[("bcA",)])
            P.emit("vector", lambda e: e.transpose(dstslice, b.bcA[:, 0:n]),
                   reads=[("bcA",)], writes=[wkey2, dregion])
        else:
            P.emit("vector", lambda e: e.tensor_copy(
                dstslice.rearrange("p (bk w) -> p bk w", w=32), rep),
                reads=[rkey], writes=[wkey2, dregion])
    expand(64, b.tfB[:, 1024:2048], ("mxsr",), ("srx",), ("tfB",))
    b.tt(w_ap, w_ap, b.tfB[:, 1024:2048], ALU.mult,
         reads=[wkey, ("srx",)], writes=[wkey])
    b.ts(w_ap, w_ap, -6.0, 6.0, ALU.max, ALU.min, reads=[wkey], writes=[wkey])
    RI = b.rows[:, 0:1024].bitcast(I32)
    b.ts(RI, w_ap.bitcast(I32), 23, None, ALU.logical_shift_right,
         reads=[wkey], writes=[("mxr",)])
    P.emit("vector", lambda e: e.tensor_scalar(RI, RI, 255, None, ALU.bitwise_and),
           reads=[("mxr",)], writes=[("mxr",)])
    b.ts(RI, RI, 127, None, ALU.subtract, reads=[("mxr",)], writes=[("mxr",)])
    b.ts(RI, RI, 0, 2, ALU.max, ALU.min, reads=[("mxr",)], writes=[("mxr",)])
    SI = b.rows[:, 1024:2048].bitcast(I32)
    b.ts(SI, RI, 126, None, ALU.add, reads=[("mxr",)], writes=[("mxs",)])
    P.emit("vector", lambda e: e.tensor_scalar(SI, SI, 23, None, ALU.logical_shift_left),
           reads=[("mxs",)], writes=[("mxs",)])
    b.ts(RI, RI, -1.0, 128.0, ALU.mult, ALU.add, reads=[("mxr",)], writes=[("mxr",)])
    P.emit("vector", lambda e: e.tensor_scalar(RI, RI, 23, None, ALU.logical_shift_left),
           reads=[("mxr",)], writes=[("mxr",)])
    b.tt(b.tfB[:, 0:n], w_ap, b.rows[:, 0:1024], ALU.mult,
         reads=[wkey, ("mxr",)], writes=[("tfB",)])
    b.ts(b.small16[:, 0:1024].bitcast(I16), b.tfB[:, 0:n], 1.0, None,
         ALU.mult, reads=[("tfB",)], writes=[("sm16",)])
    b.tt(b.tfB[:, 0:n], b.small16[:, 0:1024].bitcast(I16),
         b.rows[:, 1024:2048], ALU.mult, reads=[("sm16",), ("mxs",)], writes=[("tfB",)])
    expand(96, b.bcA[:, 1024:2048] if part_blocks else b.bcA[:, 0:n],
           ("mxsc",), ("scx",), ("bcA",))
    scx = b.bcA[:, 1024:2048] if part_blocks else b.bcA[:, 0:n]
    b.tt(out16_ap, b.tfB[:, 0:n], scx, ALU.mult,
         reads=[("tfB",), ("scx",)], writes=[outkey])


def emit_quant_L3(b: Bld):
    P, din = b.P, b.din
    for src_name, dst in (("l3_wq", b.wq3_d), ("l3_wk", b.wk3_d), ("l3_wv", b.wv3_d)):
        for d in range(NCH):
            b.dma(b.tfA[:, 0:1024], din[src_name][128 * d:128 * d + 128, :],
                  queue="qsync", writes=[("tfA",)])
            P.emit("vector", lambda e: e.reduce_max(b.small[:, 0:1], b.tfA[:, 0:1024],
                                                    axis=AX.X, apply_absolute_value=True),
                   reads=[("tfA",)], writes=[("smq",)])
            b.ts(b.small[:, 0:1], b.small[:, 0:1], 1.0 / 127.0, 1e-8, ALU.mult, ALU.max,
                 reads=[("smq",)], writes=[("smq",)])
            P.emit("vector", lambda e: e.reciprocal(b.small[:, 1:2], b.small[:, 0:1]),
                   reads=[("smq",)], writes=[("smq2",)])
            b.ts(b.tfB[:, 0:1024].bitcast(I8)[:, 0:1024], b.tfA[:, 0:1024],
                 b.small[:, 1:2], None, ALU.mult,
                 reads=[("tfA",), ("smq2",)], writes=[("tfB",)])
            b.ts(b.small16[:, 0:1024], b.tfB[:, 0:1024].bitcast(I8)[:, 0:1024],
                 b.small[:, 0:1], None, ALU.mult,
                 reads=[("tfB",), ("smq",)], writes=[("sm16",)])
            b.dma(dst[128 * d:128 * d + 128, :], b.small16[:, 0:1024],
                  queue="qout", reads=[("sm16",)], writes=[(src_name + "q", d)])
    for d in range(NCH):
        b.dma(b.tfA[:, 0:1024], din["l3_wf"][128 * d:128 * d + 128, :],
              queue="qsync", writes=[("tfA",)])
        b.dma(b.small[:, 32:64], din["l3_dw"][128 * d:128 * d + 128, :],
              queue="qsync", writes=[("dwc",)])
        b.act(b.small[:, 32:64], b.small[:, 32:64], AF.Tanh, scale=4.0,
              reads=[("dwc",)], writes=[("dwc",)])
        b.ts(b.small[:, 32:64], b.small[:, 32:64], 0.5, None, ALU.mult,
             reads=[("dwc",)], writes=[("dwc",)])
        emit_mxfp4_half(b, b.tfA[:, 0:1024], ("tfA",), b.small16[:, 0:1024],
                        ("sm16",), b.small[:, 32:64], [("dwc",)],
                        part_blocks=False, dcol_n=32)
        b.dma(b.wf3_d[128 * d:128 * d + 128, :], b.small16[:, 0:1024],
              queue="qout", reads=[("sm16",)], writes=[("wf3q", d)])


def emit_L3(b: Bld):
    P, din, pb = b.P, b.din, b.pb
    b.ln_stats("l3")
    b.make_xn()
    b.dma(b.cols[:, 37:38], din["l3_dact"].broadcast_to([128, 1]),
          queue="qsync", writes=[("dac",)])
    b.act(b.cols[:, 37:38], b.cols[:, 37:38], AF.Tanh, scale=4.0,
          reads=[("dac",)], writes=[("dac",)])
    b.ts(b.cols[:, 37:38], b.cols[:, 37:38], 0.5, None, ALU.mult,
         reads=[("dac",)], writes=[("dac",)])
    for d in range(NCH):
        b.dma(b.bcA[:, 0:T], b.scr_row[:, 0:T].broadcast_to([128, T]), queue="qsync",
              reads=[("scrrow",)], writes=[("bcA",)])
        b.dma(b.tfB[:, 0:T], b.scr_row2[:, 0:T].broadcast_to([128, T]), queue="qsync",
              reads=[("scrrow2",)], writes=[("tfB",)])
        b.xn_f32_tile(d, b.tfA[:, 0:T], ("tfA",))
        for half in range(T // 1024):
            emit_mxfp4_half(b, b.tfA[:, 1024 * half:1024 * half + 1024], ("tfA",),
                            b.b1[:, d * T + 1024 * half: d * T + 1024 * half + 1024],
                            ("xq16", d), b.cols[:, 37:38], [("dac",)],
                            part_blocks=True, dcol_n=1)
        if T < 1024:
            raise ValueError("T must be >= 1024 for L3")
    for blk in range(2):
        for d in range(NCH):
            P.emit("sync", lambda e, blk=blk, d=d: e.dma_start_transpose(
                out=b.wbuf[:, 512 * d:512 * d + 512],
                in_=b.wf3_d[blk * 512: blk * 512 + 512, 128 * d:128 * d + 128]),
                reads=[("wf3q", d)], writes=[("wbuf",)], queue="qsync")
        for o128 in range(4):
            dch = blk * 4 + o128
            for tb in range(NTB):
                bi = b.bank()
                pw = ("ps", bi)
                for d in range(NCH):
                    b.mm(pb[bi][:], b.wbuf[:, 512 * d + 128 * o128:512 * d + 128 * o128 + 128],
                         b.b1[:, d * T + 512 * tb: d * T + 512 * tb + 512],
                         start=(d == 0), stop=(d == NCH - 1),
                         reads=[("wbuf",), ("xq16", d)], writes=[pw])
                b.act(b.b2[:, dch * T + 512 * tb: dch * T + 512 * tb + 512],
                      pb[bi][:], AF.Silu, bias=b.ccol[:, 24 + dch:24 + dch + 1],
                      reads=[pw, ("bf3",)], writes=[("g16", dch)])
    for dch in range(NCH):
        b.dma(b.h_sp[128 * dch: 128 * dch + 128, 0:T], b.b2[:, dch * T:(dch + 1) * T],
              queue="qout", reads=[("g16", dch)], writes=[("gsp", dch)])

    VOFF = 0
    VW = 520 * TT
    QOFF = VW
    KOFF = VW + T
    b.P.emit("vector", lambda e: e.memset(b.b2[64:96, KOFF:KOFF + T], 1.0),
             reads=(), writes=[("kst",)] + [("g16", dd) for dd in range(NCH)])
    for grp in range(2):
        for d in range(NCH):
            P.emit("sync", lambda e, grp=grp, d=d: e.dma_start_transpose(
                out=b.wbuf[:, 512 * d:512 * d + 512],
                in_=b.wv3_d[grp * 512: grp * 512 + 512, 128 * d:128 * d + 128]),
                reads=[("l3_wvq", d)], writes=[("wbuf",)], queue="qsync")
        b.load_bias_row(din["l3_bv_row"], grp * 512, 512, ("brow",))
        for tt in range(TT):
            bi = b.bank()
            pw = ("ps", bi)
            for d in range(NCH):
                b.mm(pb[bi][:], b.xn16[:, d * T + 128 * tt: d * T + 128 * tt + 128],
                     b.wbuf[:, 512 * d:512 * d + 512],
                     start=(d == 0), stop=False,
                     reads=[("wbuf",), ("xn16", d)], writes=[pw])
            b.mm(pb[bi][:], b.onesrow[0:1, 0:128], b.brow[0:1, 0:512],
                 start=False, stop=True,
                 reads=[("onesrow",), ("brow",)], writes=[pw])
            dst3 = bass.AP(b.b2[:].tensor, VOFF + 520 * tt, [[NCH * T, 128], [65, 8], [1, 64]])
            b.P.emit("vector", lambda e, dst3=dst3, bi=bi: e.tensor_copy(
                dst3, pb[bi][:].rearrange("p (h w) -> p h w", w=64)),
                reads=[pw], writes=[("vsl", tt)])
            ones3 = bass.AP(b.b2[:].tensor, VOFF + 520 * tt + 64, [[NCH * T, 128], [65, 8], [1, 1]])
            b.P.emit("vector", lambda e, ones3=ones3: e.memset(ones3, 1.0),
                     reads=(), writes=[("vsl", tt)])
        for hg in range(2):
            for wname, woff in (("wq3_d", 0), ("wk3_d", 2048)):
                wd = getattr(b, wname)
                for d in range(NCH):
                    P.emit("sync", lambda e, wd=wd, grp=grp, hg=hg, d=d, woff=woff: e.dma_start_transpose(
                        out=b.wbuf[:, woff + 256 * d: woff + 256 * d + 256],
                        in_=wd[grp * 512 + hg * 256: grp * 512 + hg * 256 + 256,
                               128 * d:128 * d + 128]),
                        reads=[("l3_wqq", d) if woff == 0 else ("l3_wkq", d)],
                        writes=[("wbuf",)], queue="qsync")
            b.dma(b.brow[:, 0:256], din["l3_bq_row"][:, grp * 512 + hg * 256:
                                                      grp * 512 + hg * 256 + 256],
                  queue="qgp", writes=[("brow",)])
            b.dma(b.brow[:, 256:512], din["l3_bk_row"][:, grp * 512 + hg * 256:
                                                       grp * 512 + hg * 256 + 256],
                  queue="qgp", writes=[("brow",)])
            for hl in range(4):
                h = 8 * grp + 4 * hg + hl
                for woff, dstoff, boff, sc in ((0, QOFF, 0, 0.5),
                                               (2048, KOFF, 256, 0.25)):
                    for tb in range(NTB):
                        bi = b.bank()
                        pw = ("ps", bi)
                        for d in range(NCH):
                            b.mm(pb[bi][0:64, :],
                                 b.wbuf[:, woff + 256 * d + 64 * hl: woff + 256 * d + 64 * hl + 64],
                                 b.xn16[:, d * T + 512 * tb: d * T + 512 * tb + 512],
                                 start=(d == 0), stop=False,
                                 reads=[("wbuf",), ("xn16", d)], writes=[pw])
                        b.mm(pb[bi][0:64, :],
                             b.brow[0:1, boff + 64 * hl: boff + 64 * hl + 64],
                             b.onesrow[0:1, 0:512], start=False, stop=True,
                             reads=[("brow",), ("onesrow",)], writes=[pw])
                        b.ts(b.b2[0:64, dstoff + 512 * tb: dstoff + 512 * tb + 512],
                             pb[bi][0:64, :], sc, None, ALU.mult,
                             reads=[pw], writes=[("qst",) if dstoff == QOFF else ("kst",)])
                for srcoff, dstrow, kk in ((QOFF, 0, ("qst",)), (KOFF, 1, ("kst",))):
                    b.act(b.tfA[0:64, 0:T], b.b2[0:64, srcoff:srcoff + T], AF.Square,
                          reads=[kk], writes=[("tfA",)])
                    for tb in range(NTB):
                        b.mm(pb[4][0:1, 512 * (tb % 1):512 * (tb % 1) + 512]
                             if False else pb[4][0:1, :],
                             b.onescolf[0:64, :], b.tfA[0:64, 512 * tb:512 * tb + 512],
                             start=True, stop=True,
                             reads=[("tfA",), ("onescol",)], writes=[("ps", 4)])
                        dstrow_ap = b.rows[0:1, 512 * tb:512 * tb + 512] if dstrow == 0                             else b.tfB[0:1, 512 * tb:512 * tb + 512]
                        b.P.emit("vector", lambda e, a=dstrow_ap: e.tensor_copy(a, pb[4][0:1, :]),
                                 reads=[("ps", 4)], writes=[("qn2",) if dstrow == 0 else ("kn2",)])
                P.emit("vector", lambda e: e.reduce_max(b.small[0:1, 10:11], b.tfB[0:1, 0:T],
                                                        axis=AX.X),
                       reads=[("kn2",)], writes=[("mk2",)])
                b.act(b.small[0:1, 10:11], b.small[0:1, 10:11], AF.Ln,
                      reads=[("mk2",)], writes=[("mk2",)])
                b.act(b.small[0:1, 10:11], b.small[0:1, 10:11], AF.Exp, scale=0.5,
                      reads=[("mk2",)], writes=[("mk2",)])
                b.act(b.rows[0:1, 0:T], b.rows[0:1, 0:T], AF.Ln,
                      reads=[("qn2",)], writes=[("qn2",)])
                b.act(b.rows[0:1, 0:T], b.rows[0:1, 0:T], AF.Exp, scale=0.5,
                      reads=[("qn2",)], writes=[("qn2",)])
                b.ts(b.rows[0:1, 0:T], b.rows[0:1, 0:T], b.small[0:1, 10:11], -1.0,
                     ALU.mult, ALU.mult, reads=[("qn2",), ("mk2",)], writes=[("qn2",)])
                b.P.emit("vector", lambda e: e.tensor_copy(b.b2[64:65, QOFF:QOFF + T],
                                                           b.rows[0:1, 0:T]),
                         reads=[("qn2",)], writes=[("qst",)])
                for tb in range(NTB):
                    av = 2 + (tb % 2)
                    for j in range(TT):
                        sb_ = j % 2
                        pws = ("ps", sb_)
                        b.mm(pb[sb_][:], b.b2[0:65, KOFF + 128 * j: KOFF + 128 * j + 128],
                             b.b2[0:65, QOFF + 512 * tb: QOFF + 512 * tb + 512],
                             start=True, stop=True,
                             reads=[("kst",), ("qst",)], writes=[pws])
                        et = b.small16[:, 512 * (j % 2): 512 * (j % 2) + 512]
                        b.act(et, pb[sb_][:], AF.Exp,
                              reads=[pws], writes=[("ET", j % 2), ("sm16",)])
                        b.mm(pb[av][0:65, :],
                             b.b2[:, VOFF + 520 * j + 65 * (4 * hg + hl) + 65 * 0:
                                  VOFF + 520 * j + 65 * (4 * hg + hl) + 65],
                             et, start=(j == 0), stop=(j == TT - 1),
                             reads=[("vsl", j), ("ET", j % 2)], writes=[("ps", av)])
                    dchunk, rbase = divmod(64 * h, 128)
                    b.P.emit("vector", lambda e, av=av, dchunk=dchunk, rbase=rbase, tb=tb: e.tensor_copy(
                        b.b1[rbase:rbase + 64, dchunk * T + 512 * tb: dchunk * T + 512 * tb + 512],
                        pb[av][0:64, :]),
                        reads=[("ps", av)], writes=[("s16", dchunk), ("xq16", dchunk)])
                    b.P.emit("vector", lambda e, av=av, tb=tb: e.tensor_copy(
                        b.tfB[0:1, 512 * tb:512 * tb + 512], pb[av][64:65, :]),
                        reads=[("ps", av)], writes=[("z", tb)])
                P.emit("vector", lambda e: e.reciprocal(b.rows[0:1, 0:T], b.tfB[0:1, 0:T]),
                       reads=[("z", tb2) for tb2 in range(NTB)], writes=[("zr",)])
                dchunk, rbase = divmod(64 * h, 128)
                b.bcast_row(b.rows[0:1, 0:T], b.bcA[rbase:rbase + 64, 0:T], T,
                            ("zr",), ("bcZ",), parts=64, scr=b.scr_rowZ)
                b.tt(b.b1[rbase:rbase + 64, dchunk * T: dchunk * T + T],
                     b.b1[rbase:rbase + 64, dchunk * T: dchunk * T + T],
                     b.bcA[rbase:rbase + 64, 0:T], ALU.mult,
                     reads=[("s16", dchunk), ("bcZ",)], writes=[("s16", dchunk)])
    for dch in range(NCH):
        for tb in range(NTB):
            b.dma(b.small16[:, 0:512], b.h_sp[128 * dch:128 * dch + 128,
                                              512 * tb:512 * tb + 512],
                  queue="qsync", reads=[("gsp", dch)], writes=[("sm16",)])
            b.residual_sg(dch, tb, b.small16[:, 0:512], ("sm16",))


def emit_L2(b: Bld):
    P, din, pb = b.P, b.din, b.pb
    TWO_PI = 6.283185307179586
    C_HI = 6.28125
    C_LO = TWO_PI - C_HI
    b.ln_stats("l2")
    b.make_xn()
    P.emit("vector", lambda e: e.memset(b.cols[:, 40:41], 0.0), writes=[("amx", 0)])
    for d in range(NCH):
        b.dma(b.bcA[:, 0:T], b.scr_row[:, 0:T].broadcast_to([128, T]), queue="qsync",
              reads=[("scrrow",)], writes=[("bcA",)])
        b.dma(b.tfB[:, 0:T], b.scr_row2[:, 0:T].broadcast_to([128, T]), queue="qsync",
              reads=[("scrrow2",)], writes=[("tfB",)])
        b.xn_f32_tile(d, b.tfA[:, 0:T], ("tfA",))
        P.emit("vector", lambda e: e.reduce_max(b.small[:, 20:21], b.tfA[:, 0:T],
                                                axis=AX.X, apply_absolute_value=True),
               reads=[("tfA",)], writes=[("am1",)])
        b.tt(b.cols[:, 40:41], b.cols[:, 40:41], b.small[:, 20:21], ALU.max,
             reads=[("amx", 0), ("am1",)], writes=[("amx", 0)])
    b.start_amax_collective(0, ("amx", 0), 40)

    b.dma(b.small[0:1, 440:472], din["c_rope_e"].rearrange("a b -> b a"), queue="qsync",
          writes=[("rpe",)])
    b.act(b.small[0:1, 440:472], b.small[0:1, 440:472], AF.Exp,
          reads=[("rpe",)], writes=[("rpe",)])
    b.dma(b.rows[0:1, 0:T], din["c_pos_row"][:, 0:T], queue="qsync", writes=[("posr",)])
    for tb in range(NTB):
        b.mm(pb[4][0:32, :], b.small[0:1, 440:472], b.rows[0:1, 512 * tb:512 * tb + 512],
             start=True, stop=True, reads=[("rpe",), ("posr",)], writes=[("ps", 4)])
        P.emit("vector", lambda e, tb=tb: e.tensor_copy(
            b.tfA[0:32, 512 * tb:512 * tb + 512], pb[4][0:32, :]),
            reads=[("ps", 4)], writes=[("frq",)])

    def trig(dst_rows, shift, key):
        b.ts(b.tfB[0:32, 0:T], b.tfA[0:32, 0:T], shift, 1.0 / TWO_PI,
             ALU.add, ALU.mult, reads=[("frq",)], writes=[("tfB",)])
        b.ts(b.tfB[0:32, 0:T], b.tfB[0:32, 0:T], 0.5, None, ALU.add,
             reads=[("tfB",)], writes=[("tfB",)])
        ki = b.bcA[0:32, 0:T].bitcast(I16)[:, 0:T]
        P.emit("vector", lambda e, ki=ki: e.tensor_scalar(ki, b.tfB[0:32, 0:T],
                                                          1.0, None, ALU.mult),
               reads=[("tfB",)], writes=[("ki",)])
        b.stt(b.tfB[0:32, 0:T], ki, -C_HI, b.tfA[0:32, 0:T], ALU.mult, ALU.add,
              reads=[("ki",), ("frq",)], writes=[("tfB",)])
        if shift:
            b.ts(b.tfB[0:32, 0:T], b.tfB[0:32, 0:T], shift, None, ALU.add,
                 reads=[("tfB",)], writes=[("tfB",)])
        b.stt(b.tfB[0:32, 0:T], ki, -C_LO, b.tfB[0:32, 0:T], ALU.mult, ALU.add,
              reads=[("ki",), ("tfB",)], writes=[("tfB",)])
        b.act(dst_rows, b.tfB[0:32, 0:T], AF.Sin, reads=[("tfB",)], writes=[key])
    trig(b.rows[0:32, 0:T], 0.0, ("sint",))
    trig(b.rows[64:96, 0:T], math.pi / 2.0, ("cost",))
    b.dma(b.scr32[:, 0:T], b.rows[64:96, 0:T], queue="qout", reads=[("cost",)],
          writes=[("s32",)])
    b.dma(b.bcA[:, 0:T], bass.AP(b.scr32[:].tensor, 0, [[0, 4], [2048, 32], [1, T]]),
          queue="qsync", reads=[("s32",)], writes=[("bcA",)])
    b.ts(b.rows[32:64, 0:T], b.rows[0:32, 0:T], -1.0, None, ALU.mult,
         reads=[("sint",)], writes=[("nsint",)])
    b.dma(b.scr64[0:32, 0:T], b.rows[32:64, 0:T], queue="qout", reads=[("nsint",)],
          writes=[("s64",)])
    b.dma(b.scr64[32:64, 0:T], b.rows[0:32, 0:T], queue="qout", reads=[("sint",)],
          writes=[("s64",)])
    b.dma(b.tfB[:, 0:T], bass.AP(b.scr64[:].tensor, 0, [[0, 2], [2048, 64], [1, T]]),
          queue="qsync", reads=[("s64",)], writes=[("tfB",)])
    if hasattr(b, "dbg_hook"):
        b.dbg_hook(b, "trig")
    b.dma(b.ccol[:, 32:40], din["l2_sqk_cc"], queue="qsync", writes=[("sqk",)])
    b.vcopy(b.small16[:, 0:8], b.ccol[:, 32:40], reads=[("sqk",)], writes=[("sm16",)])
    P.emit("tensor", lambda e: e.transpose(b.ptr[0:8, 0:128], b.small16[:, 0:8],
                                           b.ident[:]),
           reads=[("sm16",), ("ident",)], writes=[("ptr",)])
    P.emit("vector", lambda e: e.reduce_max(
        b.small[0:8, 22:24].unsqueeze(2),
        b.ptr[0:8, 0:128].rearrange("p (h w) -> p h w", w=64), axis=AX.X,
        apply_absolute_value=True),
        reads=[("ptr",)], writes=[("bhx",)])
    b.tt(b.small[0:8, 22:24], b.small[0:8, 22:24], b.small[0:8, 22:24], ALU.mult,
         reads=[("bhx",)], writes=[("bhx",)])
    b.ts(b.small[0:8, 22:24], b.small[0:8, 22:24], -8.0, None, ALU.mult,
         reads=[("bhx",)], writes=[("bhx",)])
    b.dma(b.scr_col[0:1, 0:16].rearrange("one (c hh) -> (one c) hh", hh=2),
          b.small[0:8, 22:24], queue="qout", reads=[("bhx",)], writes=[("scrcol",)])
    for h in range(16):
        b.dma(b.cols[:, 44 + h:45 + h],
              b.scr_col[0:1, h:h + 1].broadcast_to([128, 1]),
              queue="qsync", reads=[("scrcol",)], writes=[("nB", h)])

    for hgi in range(4):
        for wname, base, slabkey, sc in (("l2_wq_t", 0, "qsl", 4.0),
                                         ("l2_wk_t", 2 * T, "ksl", 2.0)):
            b.load_wt_block(din[wname], 256 * hgi, 256, ("wbuf",))
            for oc in range(2):
                cglob = 2 * hgi + oc
                for tb in range(NTB):
                    bi = b.bank()
                    pw = ("ps", bi)
                    for d in range(NCH):
                        b.mm(pb[bi][:], b.wbuf[:, 512 * d + 128 * oc: 512 * d + 128 * oc + 128],
                             b.xn16[:, d * T + 512 * tb: d * T + 512 * tb + 512],
                             start=(d == 0), stop=(d == NCH - 1),
                             reads=[("wbuf",), ("xn16", d)], writes=[pw])
                    dst = b.b1[:, base + oc * T + 512 * tb: base + oc * T + 512 * tb + 512]
                    P.emit("vector", lambda e, dst=dst, bi=bi: e.tensor_copy(dst, pb[bi][:]),
                           reads=[pw], writes=[(slabkey, oc)])
                base2 = base + oc * T
                for swp in range(4):
                    srcr = (swp // 2) * 64 + (1 - swp % 2) * 32
                    dstr = (swp // 2) * 64 + (swp % 2) * 32
                    P.emit("vector", lambda e, srcr=srcr, dstr=dstr, base2=base2: e.tensor_copy(
                        b.tfA[dstr:dstr + 32, 0:T], b.b1[srcr:srcr + 32, base2:base2 + T]),
                        reads=[(slabkey, oc)], writes=[("swp",)])
                b.tt(b.b1[:, base2:base2 + T], b.b1[:, base2:base2 + T], b.bcA[:, 0:T],
                     ALU.mult, reads=[(slabkey, oc), ("bcA",)], writes=[(slabkey, oc)])
                P.emit("vector", lambda e: e.tensor_tensor(
                    b.tfA[:, 0:T], b.tfA[:, 0:T], b.tfB[:, 0:T], ALU.mult),
                    reads=[("swp",), ("tfB",)], writes=[("swp",)])
                b.tt(b.b1[:, base2:base2 + T], b.b1[:, base2:base2 + T], b.tfA[:, 0:T],
                     ALU.add, reads=[(slabkey, oc), ("swp",)], writes=[(slabkey, oc)])
                b.act(b.tfA[:, 0:T], b.b1[:, base2:base2 + T], AF.Square,
                      reads=[(slabkey, oc)], writes=[("swp",)])
                for hh in range(2):
                    for tb in range(NTB):
                        b.mm(pb[4][0:1, :], b.onescolf[64 * hh:64 * hh + 64, :],
                             b.tfA[64 * hh:64 * hh + 64, 512 * tb:512 * tb + 512],
                             start=True, stop=True, reads=[("swp",), ("onescol",)],
                             writes=[("ps", 4)])
                        rr = 0 if hh == 0 else 32
                        P.emit("vector", lambda e, rr=rr, tb=tb: e.tensor_copy(
                            b.rows[rr:rr + 1, 512 * tb:512 * tb + 512], pb[4][0:1, :]),
                            reads=[("ps", 4)], writes=[("n2", hh)])
                for hh, rr in ((0, 0), (1, 32)):
                    b.act(b.rows[rr:rr + 1, 0:T], b.rows[rr:rr + 1, 0:T], AF.Ln,
                          reads=[("n2", hh)], writes=[("n2", hh)])
                    b.act(b.rows[rr:rr + 1, 0:T], b.rows[rr:rr + 1, 0:T], AF.Exp, scale=-0.5,
                          reads=[("n2", hh)], writes=[("n2", hh)])
                b.dma(b.scr64[0:1, 0:T], b.rows[0:1, 0:T], queue="qout",
                      reads=[("n2", 0)], writes=[("s64b",)])
                b.dma(b.scr64[1:2, 0:T], b.rows[32:33, 0:T], queue="qout",
                      reads=[("n2", 1)], writes=[("s64b",)])
                b.dma(b.tfA[:, 0:T], bass.AP(b.scr64[:].tensor, 0, [[2048, 2], [0, 64], [1, T]]),
                      queue="qsync", reads=[("s64b",)], writes=[("swp",)])
                b.tt(b.b1[:, base2:base2 + T], b.b1[:, base2:base2 + T], b.tfA[:, 0:T],
                     ALU.mult, reads=[(slabkey, oc), ("swp",)], writes=[(slabkey, oc)])
                b.ts(b.b1[:, base2:base2 + T], b.b1[:, base2:base2 + T],
                     b.ccol[:, 32 + cglob:33 + cglob], sc, ALU.mult, ALU.mult,
                     reads=[(slabkey, oc), ("sqk",)], writes=[(slabkey, oc)])
        b.load_wt_block(din["l2_wv_t"], 256 * hgi, 256, ("wbuf",))
        for tt in range(TT):
            bi = b.bank()
            pw = ("ps", bi)
            for d in range(NCH):
                b.mm(pb[bi][:, 0:256], b.xn16[:, d * T + 128 * tt: d * T + 128 * tt + 128],
                     b.wbuf[:, 512 * d:512 * d + 256],
                     start=(d == 0), stop=(d == NCH - 1),
                     reads=[("wbuf",), ("xn16", d)], writes=[pw])
            dst3 = bass.AP(b.b2[:].tensor, 260 * tt, [[NCH * T, 128], [65, 4], [1, 64]])
            P.emit("vector", lambda e, dst3=dst3, bi=bi: e.tensor_copy(
                dst3, pb[bi][:, 0:256].rearrange("p (h w) -> p h w", w=64)),
                reads=[pw], writes=[("vslL2", tt)])
            ones3 = bass.AP(b.b2[:].tensor, 260 * tt + 64, [[NCH * T, 128], [65, 4], [1, 1]])
            P.emit("vector", lambda e, ones3=ones3: e.memset(ones3, 1.0),
                   reads=(), writes=[("vslL2", tt)])
        if hgi == 0:
            P.emit("vector", lambda e: e.memset(b.b1[64:96, 7 * T:8 * T], 1.0),
                   reads=(), writes=[("kst2",)])
        for ll in range(4):
            h = 4 * hgi + ll
            qb = (ll // 2) * T
            rb = 64 * (ll % 2)
            for tt in range(TT):
                bi = b.bank()
                pw = ("ps", bi)
                for sc4 in range(NTB):
                    b.mm(pb[bi][:], b.b1[rb:rb + 64, qb + 128 * tt: qb + 128 * tt + 128],
                         b.b1[rb:rb + 64, 2 * T + qb + 512 * sc4: 2 * T + qb + 512 * sc4 + 512],
                         start=True, stop=True,
                         reads=[("qsl", ll // 2), ("ksl", ll // 2)], writes=[pw])
                    if sc4 == 0:
                        P.emit("vector", lambda e, bi=bi, tt=tt: e.reduce_max(
                            b.small[:, 200 + tt:201 + tt], pb[bi][:], axis=AX.X),
                            reads=[pw], writes=[("rmx", tt)])
                    else:
                        P.emit("vector", lambda e, bi=bi: e.reduce_max(
                            b.small[:, 216:217], pb[bi][:], axis=AX.X),
                            reads=[pw], writes=[("rmx1",)])
                        b.tt(b.small[:, 200 + tt:201 + tt], b.small[:, 200 + tt:201 + tt],
                             b.small[:, 216:217], ALU.max,
                             reads=[("rmx", tt), ("rmx1",)], writes=[("rmx", tt)])
            P.emit("tensor", lambda e: e.transpose(pb[4][0:TT, 0:128],
                                                   b.small[:, 200:200 + TT], b.identf[:]),
                   reads=[("identf",)] + [("rmx", t2) for t2 in range(TT)],
                   writes=[("ps", 4)])
            P.emit("vector", lambda e: e.tensor_scalar(
                b.small[0:TT, 216:344], pb[4][0:TT, 0:128], -1.0, None, ALU.mult),
                reads=[("ps", 4)], writes=[("nrm",)])
            b.dma(b.scr_rowZ[0:1, 0:T].rearrange("one (tt p) -> (one tt) p", p=128),
                  b.small[0:TT, 216:344], queue="qout",
                  reads=[("nrm",)], writes=[("scrrowZ",)])
            b.dma(b.b1[64:65, 6 * T:7 * T], b.scr_rowZ[0:1, 0:T], queue="qgp",
                  reads=[("scrrowZ",)], writes=[("qst2",)])
            P.emit("vector", lambda e, rb=rb, qb=qb: e.tensor_copy(
                b.b1[0:64, 6 * T:7 * T], b.b1[rb:rb + 64, qb:qb + T]),
                reads=[("qsl", ll // 2)], writes=[("qst2",)])
            P.emit("vector", lambda e, rb=rb, qb=qb: e.tensor_copy(
                b.b1[0:64, 7 * T:8 * T], b.b1[rb:rb + 64, 2 * T + qb:2 * T + qb + T]),
                reads=[("ksl", ll // 2)], writes=[("kst2",)])
            for tb in range(NTB):
                av = 2 + (tb % 2)
                for j in range(TT):
                    sb_ = j % 2
                    pws = ("ps", sb_)
                    b.mm(pb[sb_][:], b.b1[0:65, 7 * T + 128 * j: 7 * T + 128 * j + 128],
                         b.b1[0:65, 6 * T + 512 * tb: 6 * T + 512 * tb + 512],
                         start=True, stop=True,
                         reads=[("kst2",), ("qst2",)], writes=[pws])
                    et = b.small16[:, 512 * (j % 2): 512 * (j % 2) + 512]
                    b.act(et, pb[sb_][:], AF.Exp, scale=1024.0,
                          reads=[pws], writes=[("ET", j % 2), ("sm16",)])
                    b.mm(pb[av][0:65, :],
                         b.b2[:, 260 * j + 65 * ll: 260 * j + 65 * ll + 65],
                         et, start=(j == 0), stop=(j == TT - 1),
                         reads=[("vslL2", j), ("ET", j % 2)], writes=[("ps", av)])
                dchl = (ll // 2)
                P.emit("vector", lambda e, av=av, dchl=dchl, rb=rb, tb=tb: e.tensor_copy(
                    b.b1[rb:rb + 64, 4 * T + dchl * T + 512 * tb: 4 * T + dchl * T + 512 * tb + 512],
                    pb[av][0:64, :]),
                    reads=[("ps", av)], writes=[("cxs", dchl)])
                P.emit("vector", lambda e, av=av, tb=tb: e.tensor_copy(
                    b.rows[96:97, 512 * tb:512 * tb + 512], pb[av][64:65, :]),
                    reads=[("ps", av)], writes=[("z", tb)])
            if hasattr(b, "dbg_hook"):
                b.dbg_hook(b, ("prediv", hgi, ll, rb))
            P.emit("vector", lambda e: e.reciprocal(b.rows[96:97, 0:T], b.rows[96:97, 0:T]),
                   reads=[("z", tb2) for tb2 in range(NTB)], writes=[("zr",)])
            b.bcast_row(b.rows[96:97, 0:T], b.tfA[rb:rb + 64, 0:T], T,
                        ("zr",), ("bcZt",), parts=64, scr=b.scr_rowZ)
            b.tt(b.b1[rb:rb + 64, 4 * T + (ll // 2) * T: 4 * T + (ll // 2) * T + T],
                 b.b1[rb:rb + 64, 4 * T + (ll // 2) * T: 4 * T + (ll // 2) * T + T],
                 b.tfA[rb:rb + 64, 0:T], ALU.mult,
                 reads=[("cxs", ll // 2), ("bcZt",)], writes=[("cxs", ll // 2)])
        for cc2 in range(2):
            b.dma(b.ctx_sp[256 * hgi + 128 * cc2: 256 * hgi + 128 * cc2 + 128, 0:T],
                  b.b1[:, 4 * T + cc2 * T: 4 * T + cc2 * T + T],
                  queue="qout", reads=[("cxs", cc2)], writes=[("ctxsp", hgi, cc2)])

    if hasattr(b, "dbg_hook"):
        b.dbg_hook(b, "preout")
    P.full_fence()
    for d in range(NCH):
        b.dma(b.b2[:, d * T:(d + 1) * T], b.ctx_sp[128 * d:128 * d + 128, 0:T],
              queue="qsync", writes=[("ctx", d)])
    for blk in range(2):
        b.load_wt_block(din["l2_wo_t"], blk * 512, 512, ("wbuf",))
        for o128 in range(4):
            dch = blk * 4 + o128
            for tb in range(NTB):
                bi = b.bank()
                pw = ("ps", bi)
                for d in range(NCH):
                    b.mm(pb[bi][:], b.wbuf[:, 512 * d + 128 * o128:512 * d + 128 * o128 + 128],
                         b.b2[:, d * T + 512 * tb: d * T + 512 * tb + 512],
                         start=(d == 0), stop=(d == NCH - 1),
                         reads=[("wbuf",), ("ctx", d)], writes=[pw])
                dst = b.b1[:, dch * T + 512 * tb: dch * T + 512 * tb + 512]
                P.emit("vector", lambda e, dst=dst, bi=bi: e.tensor_copy(dst, pb[bi][:]),
                       reads=[pw], writes=[("s16", dch)])

    P.full_fence()
    b.finish_amax_scale(0, 60, 61)
    for d in range(NCH):
        b.dma(b.bcA[:, 0:T], b.scr_row[:, 0:T].broadcast_to([128, T]), queue="qsync",
              reads=[("scrrow",)], writes=[("bcA",)])
        b.dma(b.tfB[:, 0:T], b.scr_row2[:, 0:T].broadcast_to([128, T]), queue="qsync",
              reads=[("scrrow2",)], writes=[("tfB",)])
        b.xn_f32_tile(d, b.tfA[:, 0:T], ("tfA",))
        ti = b.rows[:, 0:1024].bitcast(I16)[:, 0:1024]
        for half in range(T // 1024):
            P.emit("vector", lambda e, ti=ti, half=half: e.tensor_scalar(
                ti, b.tfA[:, 1024 * half:1024 * half + 1024],
                b.cols[:, 61:62], None, ALU.mult),
                reads=[("tfA",), ("sc", 61)], writes=[("ti",)])
            P.emit("vector", lambda e, ti=ti, half=half, d=d: e.tensor_scalar(
                b.xn16[:, d * T + 1024 * half: d * T + 1024 * half + 1024], ti,
                b.cols[:, 60:61], None, ALU.mult),
                reads=[("ti",), ("sc", 60)], writes=[("xnq", d)])
    P.emit("vector", lambda e: e.memset(b.cols[:, 41:43], 0.0), writes=[("amx", 1), ("amx", 2)])
    for which, wdram, spill, amxc in (("up", b.wuq_d, b.up_sp, 41),
                                      ("gate", b.wgq_d, b.gate_sp, 42)):
        for blk in range(6):
            for d in range(NCH):
                P.emit("sync", lambda e, wdram=wdram, blk=blk, d=d: e.dma_start_transpose(
                    out=b.wbuf[:, 512 * d:512 * d + 512],
                    in_=wdram[blk * 512: blk * 512 + 512, 128 * d:128 * d + 128]),
                    reads=[(which + "q",)], writes=[("wbuf",)], queue="qsync")
            for o128 in range(4):
                och = 4 * blk + o128
                for tb in range(NTB):
                    bi = b.bank()
                    pw = ("ps", bi)
                    for d in range(NCH):
                        b.mm(pb[bi][:], b.wbuf[:, 512 * d + 128 * o128:512 * d + 128 * o128 + 128],
                             b.xn16[:, d * T + 512 * tb: d * T + 512 * tb + 512],
                             start=(d == 0), stop=(d == NCH - 1),
                             reads=[("wbuf",), ("xnq", d)], writes=[pw])
                    P.emit("vector", lambda e, bi=bi: e.tensor_copy(
                        b.small16[:, 0:512], pb[bi][:]),
                        reads=[pw], writes=[("sm16",)])
                    b.dma(spill[128 * och:128 * och + 128, 512 * tb:512 * tb + 512],
                          b.small16[:, 0:512], queue="qout",
                          reads=[("sm16",)], writes=[(which + "sp", och)])
                    P.emit("vector", lambda e, bi=bi: e.reduce_max(
                        b.small[:, 20:21], pb[bi][:], axis=AX.X, apply_absolute_value=True),
                        reads=[pw], writes=[("am1",)])
                    b.tt(b.cols[:, amxc:amxc + 1], b.cols[:, amxc:amxc + 1],
                         b.small[:, 20:21], ALU.max,
                         reads=[("amx", amxc - 40), ("am1",)], writes=[("amx", amxc - 40)])
    b.start_amax_collective(1, ("amx", 1), 41, n=2)
    b.finish_amax_scale(1, 62, 63)
    b.finish_amax_scale(2, 64, 65)
    b.ts(b.cols[:, 66:67], b.cols[:, 64:65], 1.0 / 65536.0, None, ALU.mult,
         reads=[("sc", 64)], writes=[("sc", 66)])
    P.emit("vector", lambda e: e.memset(b.cols[:, 43:44], 0.0), writes=[("amx", 3)])
    for och in range(24):
        for tb in range(NTB):
            b.dma(b.small16[:, 0:512], b.gate_sp[128 * och:128 * och + 128,
                                                 512 * tb:512 * tb + 512],
                  queue="qsync", reads=[("gatesp", och)], writes=[("sm16",)])
            gi = b.rows[:, 0:512].bitcast(I16)[:, 0:512]
            P.emit("vector", lambda e, gi=gi: e.tensor_scalar(
                gi, b.small16[:, 0:512], b.cols[:, 65:66], None, ALU.mult),
                reads=[("sm16",), ("sc", 65)], writes=[("gi",)])
            b.act(b.tfA[:, 0:512], gi, AF.Sigmoid, scale=b.cols[:, 64:65],
                  reads=[("gi",), ("sc", 64)], writes=[("sig",)])
            sgi = b.rows[:, 512:1024].bitcast(I32)[:, 0:512]
            P.emit("vector", lambda e, sgi=sgi: e.tensor_scalar(
                sgi, b.tfA[:, 0:512], 65536.0, None, ALU.mult),
                reads=[("sig",)], writes=[("sigi",)])
            P.emit("vector", lambda e, sgi=sgi: e.tensor_scalar(
                sgi, sgi, 65535, None, ALU.min),
                reads=[("sigi",)], writes=[("sigi",)])
            b.stt(b.tfA[:, 0:512], sgi, b.cols[:, 66:67], gi, ALU.mult, ALU.mult,
                  reads=[("sigi",), ("gi",), ("sc", 66)], writes=[("tv",)])
            P.emit("vector", lambda e: e.reduce_max(
                b.small[:, 20:21], b.tfA[:, 0:512], axis=AX.X, apply_absolute_value=True),
                reads=[("tv",)], writes=[("am1",)])
            b.tt(b.cols[:, 43:44], b.cols[:, 43:44], b.small[:, 20:21], ALU.max,
                 reads=[("amx", 3), ("am1",)], writes=[("amx", 3)])
            P.emit("vector", lambda e: e.tensor_copy(b.small16[:, 512:1024], b.tfA[:, 0:512]),
                   reads=[("tv",)], writes=[("sm16",)])
            b.dma(b.t_sp[128 * och:128 * och + 128, 512 * tb:512 * tb + 512],
                  b.small16[:, 512:1024], queue="qout",
                  reads=[("sm16",)], writes=[("tsp", och)])
    b.start_amax_collective(3, ("amx", 3), 43)
    b.finish_amax_scale(3, 67, 68)
    b.ts(b.cols[:, 69:70], b.cols[:, 67:68], 1.0, None, ALU.mult,
         reads=[("sc", 67)], writes=[("sc", 69)])
    b.tt(b.cols[:, 69:70], b.cols[:, 67:68], b.cols[:, 62:63], ALU.mult,
         reads=[("sc", 67), ("sc", 62)], writes=[("sc", 69)])
    P.emit("vector", lambda e: e.memset(b.cols[:, 70:71], 0.0), writes=[("amx", 4)])
    for och in range(24):
        for tb in range(NTB):
            b.dma(b.small16[:, 0:512], b.t_sp[128 * och:128 * och + 128,
                                              512 * tb:512 * tb + 512],
                  queue="qsync", reads=[("tsp", och)], writes=[("sm16",)])
            b.dma(b.small16[:, 512:1024], b.up_sp[128 * och:128 * och + 128,
                                                  512 * tb:512 * tb + 512],
                  queue="qsync", reads=[("upsp", och)], writes=[("sm16",)])
            g2 = b.rows[:, 0:512].bitcast(I16)[:, 0:512]
            ui = b.rows[:, 1024:1536].bitcast(I16)[:, 0:512]
            P.emit("vector", lambda e, g2=g2: e.tensor_scalar(
                g2, b.small16[:, 0:512], b.cols[:, 68:69], None, ALU.mult),
                reads=[("sm16",), ("sc", 68)], writes=[("gi",)])
            P.emit("vector", lambda e, ui=ui: e.tensor_scalar(
                ui, b.small16[:, 512:1024], b.cols[:, 63:64], None, ALU.mult),
                reads=[("sm16",), ("sc", 63)], writes=[("ui",)])
            b.stt(b.tfA[:, 0:512], g2, b.cols[:, 69:70], ui, ALU.mult, ALU.mult,
                  reads=[("gi",), ("ui",), ("sc", 69)], writes=[("tv",)])
            P.emit("vector", lambda e: e.reduce_max(
                b.small[:, 20:21], b.tfA[:, 0:512], axis=AX.X, apply_absolute_value=True),
                reads=[("tv",)], writes=[("am1",)])
            b.tt(b.cols[:, 70:71], b.cols[:, 70:71], b.small[:, 20:21], ALU.max,
                 reads=[("amx", 4), ("am1",)], writes=[("amx", 4)])
            P.emit("vector", lambda e: e.tensor_copy(b.small16[:, 512:1024], b.tfA[:, 0:512]),
                   reads=[("tv",)], writes=[("sm16",)])
            b.dma(b.gate_sp[128 * och:128 * och + 128, 512 * tb:512 * tb + 512],
                  b.small16[:, 512:1024], queue="qout",
                  reads=[("sm16",)], writes=[("zsp", och)])
    b.start_amax_collective(4, ("amx", 4), 70)
    b.finish_amax_scale(4, 71, 72)
    P.full_fence()
    KP = 24 if NCH * T >= 24 * 512 + 512 else 12
    npanel = (24 + KP - 1) // KP
    for ohalf in range(2):
        for tb in range(NTB):
            partial_keys = []
            for pan in range(npanel):
                c0 = pan * KP
                nch = min(KP, 24 - c0)
                for c in range(nch):
                    P.emit("sync", lambda e, ohalf=ohalf, c=c, c0=c0: e.dma_start_transpose(
                        out=b.xn16[:, 512 * c:512 * c + 512],
                        in_=b.wdq_d[ohalf * 512: ohalf * 512 + 512,
                                    128 * (c0 + c):128 * (c0 + c) + 128]),
                        reads=[("downq",)], writes=[("wdn",)], queue="qsync")
                for c in range(nch):
                    b.dma(b.small16[:, 0:512], b.gate_sp[128 * (c0 + c):128 * (c0 + c) + 128,
                                                         512 * tb:512 * tb + 512],
                          queue="qsync", reads=[("zsp", c0 + c)], writes=[("sm16",)])
                    zi = b.rows[:, 0:512].bitcast(I16)[:, 0:512]
                    P.emit("vector", lambda e, zi=zi: e.tensor_scalar(
                        zi, b.small16[:, 0:512], b.cols[:, 72:73], None, ALU.mult),
                        reads=[("sm16",), ("sc", 72)], writes=[("zi",)])
                    P.emit("vector", lambda e, zi=zi, c=c: e.tensor_scalar(
                        b.b2[:, 512 * c:512 * c + 512], zi, b.cols[:, 71:72], None, ALU.mult),
                        reads=[("zi",), ("sc", 71)], writes=[("oq", c)])
                for o128 in range(4):
                    dch = 4 * ohalf + o128
                    bi = b.bank()
                    pw = ("ps", bi)
                    for c in range(nch):
                        b.mm(pb[bi][:], b.xn16[:, 512 * c + 128 * o128: 512 * c + 128 * o128 + 128],
                             b.b2[:, 512 * c:512 * c + 512],
                             start=(c == 0), stop=(c == nch - 1),
                             reads=[("wdn",), ("oq", c)], writes=[pw])
                    if npanel == 1:
                        b.act(b.small16[:, 0:512], pb[bi][:], AF.Silu,
                              reads=[pw], writes=[("sm16",)])
                        b.residual_sg(dch, tb, b.small16[:, 0:512], ("sm16",))
                    elif pan == 0:
                        P.emit("vector", lambda e, bi=bi, o128=o128: e.tensor_copy(
                            b.tfA[:, 512 * o128:512 * o128 + 512], pb[bi][:]),
                            reads=[pw], writes=[("dpart", o128)])
                    else:
                        P.emit("vector", lambda e, bi=bi, o128=o128: e.tensor_tensor(
                            b.tfA[:, 512 * o128:512 * o128 + 512],
                            b.tfA[:, 512 * o128:512 * o128 + 512], pb[bi][:], ALU.add),
                            reads=[pw, ("dpart", o128)], writes=[("dpart", o128)])
            if npanel > 1:
                for o128 in range(4):
                    dch = 4 * ohalf + o128
                    b.act(b.small16[:, 0:512], b.tfA[:, 512 * o128:512 * o128 + 512],
                          AF.Silu, reads=[("dpart", o128)], writes=[("sm16",)])
                    b.residual_sg(dch, tb, b.small16[:, 0:512], ("sm16",))


def emit_quant_L2(b: Bld):
    P, din = b.P, b.din
    jobs = (("l2_wgate", b.wgq_d, 24, 1024, "gateq"),
            ("l2_wup", b.wuq_d, 24, 1024, "upq"),
            ("l2_wdown", b.wdq_d, 8, 3072, "downq"))
    for srcn, dstd, nchunk, I, outk in jobs:
        for ch in range(nchunk):
            for sb0 in range(0, I, 1024):
                b.dma(b.tfA[:, 0:1024], din[srcn][128 * ch:128 * ch + 128, sb0:sb0 + 1024],
                      queue="qsync", writes=[("tfA",)])
                b.act(b.tfB[:, 0:1024], b.tfA[:, 0:1024], AF.Abs,
                      reads=[("tfA",)], writes=[("tfB",)])
                P.emit("vector", lambda e: e.reduce_max(
                    b.small[:, 64:96].unsqueeze(2),
                    b.tfB[:, 0:1024].rearrange("p (bk w) -> p bk w", w=32), axis=AX.X),
                    reads=[("tfB",)], writes=[("qbm",)])
                b.ts(b.small[:, 64:96], b.small[:, 64:96], 1.0 / 7.0, 1e-12,
                     ALU.mult, ALU.max, reads=[("qbm",)], writes=[("qbm",)])
                P.emit("vector", lambda e: e.reciprocal(b.small[:, 96:128], b.small[:, 64:96]),
                       reads=[("qbm",)], writes=[("qbr",)])
                rsx = bass.AP(b.small[:].tensor, 96, [[512, 128], [1, 32], [0, 32]])
                P.emit("vector", lambda e, rsx=rsx: e.tensor_tensor(
                    b.tfA[:, 0:1024].rearrange("p (bk w) -> p bk w", w=32),
                    b.tfA[:, 0:1024].rearrange("p (bk w) -> p bk w", w=32), rsx, ALU.mult),
                    reads=[("tfA",), ("qbr",)], writes=[("tfA",)])
                ti8 = b.rows[:, 0:256].bitcast(I8)[:, 0:1024]
                P.emit("vector", lambda e, ti8=ti8: e.tensor_scalar(
                    ti8, b.tfA[:, 0:1024], 1.0, None, ALU.mult),
                    reads=[("tfA",)], writes=[("qti",)])
                sx = bass.AP(b.small[:].tensor, 64, [[512, 128], [1, 32], [0, 32]])
                P.emit("vector", lambda e, ti8=ti8, sx=sx: e.tensor_tensor(
                    b.small16[:, 0:1024].rearrange("p (bk w) -> p bk w", w=32),
                    ti8.rearrange("p (bk w) -> p bk w", w=32), sx, ALU.mult),
                    reads=[("qti",), ("qbm",)], writes=[("sm16",)])
                b.dma(dstd[128 * ch:128 * ch + 128, sb0:sb0 + 1024], b.small16[:, 0:1024],
                      queue="qout", reads=[("sm16",)], writes=[(outk,)])


def emit_quant_L1(b: Bld):
    P, din, pb = b.P, b.din, b.pb
    for wt, wdram, mcol, nch, wkey in ((din["l1_w1_t"], b.w1q_d, 38, 8, "w1qd"),
                                       (din["l1_w2_t"], b.w2q_d, 39, 32, "w2qd")):
        wid = 4096 // nch * 0 + (4096 if nch == 8 else 1024)
        nq = wid // 1024
        tot = float(nch * 128) * float(wid)
        for d in range(nch):
            for q in range(nq):
                ev = (d * nq + q) % 2 == 0
                ld = b.tfA[:, 0:1024] if ev else b.bcA[:, 0:1024]
                lk = ("tfA",) if ev else ("bcA",)
                ab = b.small16[:, 0:1024] if ev else b.b1[:, 0:1024]
                ak = ("sm16",) if ev else ("qtmp16",)
                b.dma(ld, wt[128 * d:128 * d + 128, 1024 * q:1024 * q + 1024],
                      queue="qsync", writes=[lk])
                b.act(ab, ld, AF.Abs, reads=[lk], writes=[ak])
                for half in range(2):
                    b.mm(pb[5][0:1, :], b.onescol[:],
                         ab[:, 512 * half:512 * half + 512],
                         start=(d == 0 and q == 0 and half == 0),
                         stop=(d == nch - 1 and q == nq - 1 and half == 1),
                         reads=[ak, ("onescol",)], writes=[("ps", 5)])
        P.emit("vector", lambda e: e.reduce_sum(b.small[0:1, 0:1], pb[5][0:1, :],
                                                axis=AX.X),
               reads=[("ps", 5)], writes=[("qdsum",)])
        b.ts(b.small[0:1, 2:3], b.small[0:1, 0:1], 1.0 / tot, 1e-5, ALU.mult, ALU.max,
             reads=[("qdsum",)], writes=[("l1m1s",)])
        b.bcast_col(b.small[0:1, 2:3], b.cols[:, mcol:mcol + 1],
                    ("l1m1s",), ("l1m1", mcol))
        b.ts(b.small[0:1, 0:1], b.small[0:1, 2:3], 0.5, None, ALU.mult,
             reads=[("l1m1s",)], writes=[("qdsum",)])
        b.ts(b.small[0:1, 1:2], b.small[0:1, 0:1], -1.0, None, ALU.mult,
             reads=[("qdsum",)], writes=[("qdsumn",)])
        b.bcast_col(b.small[0:1, 0:1], b.cols[:, 34:35], ("qdsum",), ("dcol",))
        b.bcast_col(b.small[0:1, 1:2], b.cols[:, 35:36], ("qdsumn",), ("ndcol",))
        for d in range(nch):
            for q in range(nq):
                ev = (d * nq + q) % 2 == 0
                ld = b.tfA[:, 0:1024] if ev else b.bcA[:, 0:1024]
                lk = ("tfA",) if ev else ("bcA",)
                sc = b.tfB if ev else b.rows
                sk = ("tfB",) if ev else ("rows", 0)
                ab = b.small16[:, 0:1024] if ev else b.b1[:, 0:1024]
                ak = ("sm16",) if ev else ("qtmp16",)
                b.dma(ld, wt[128 * d:128 * d + 128, 1024 * q:1024 * q + 1024],
                      queue="qsync", writes=[lk])
                b.ts(sc[:, 0:1024], ld, b.cols[:, 34:35], None,
                     ALU.is_gt, reads=[lk, ("dcol",)], writes=[sk])
                b.ts(sc[:, 1024:2048], ld, b.cols[:, 35:36], None,
                     ALU.is_lt, reads=[lk, ("ndcol",), sk], writes=[sk])
                b.tt(ab, sc[:, 0:1024], sc[:, 1024:2048],
                     ALU.subtract, reads=[sk], writes=[ak])
                b.dma(wdram[128 * d:128 * d + 128, 1024 * q:1024 * q + 1024],
                      ab, queue="qout", reads=[ak], writes=[(wkey, d, q)])


def emit_L1(b: Bld):
    P, din, pb = b.P, b.din, b.pb
    BAS, SLAB, SLAB2, E0 = 0, 2 * T, 3 * T, 4 * T
    CTX = 4608
    b.ln_stats("l1")
    b.make_xn()

    P.emit("vector", lambda e: e.memset(b.rows[0:64, 0:T], 0.0),
           writes=[("l1mean",), ("l1a",)])
    b.load_wt_block(din["l1_wspan_t"], 0, 64, ("wbuf",))
    for tb in range(NTB):
        for d in range(NCH):
            b.mm(pb[4][0:64, :], b.wbuf[:, 512 * d:512 * d + 64],
                 b.xn16[:, d * T + 512 * tb: d * T + 512 * tb + 512],
                 start=(d == 0), stop=(d == NCH - 1),
                 reads=[("wbuf",), ("xn16", d)], writes=[("ps", 4)])
        b.act(b.rows[0:16, 512 * tb:512 * tb + 512], pb[4][0:16, :], AF.Sigmoid,
              reads=[("ps", 4)], writes=[("l1mean",)])
        b.act(b.rows[32:48, 512 * tb:512 * tb + 512], pb[4][32:48, :], AF.Exp,
              reads=[("ps", 4)], writes=[("l1a",)])
    b.ts(b.rows[0:16, 0:T], b.rows[0:16, 0:T], float(T), None, ALU.mult,
         reads=[("l1mean",)], writes=[("l1mean",)])
    b.act(b.rows[32:48, 0:T], b.rows[32:48, 0:T], AF.Ln,
          bias=b.onescolf[32:48, 0:1],
          reads=[("l1a",), ("onescol",)], writes=[("l1a",)])
    b.ts(b.rows[32:48, 0:T], b.rows[32:48, 0:T], 1e-12, None, ALU.max,
         reads=[("l1a",)], writes=[("l1a",)])
    b.act(b.rows[32:48, 0:T], b.rows[32:48, 0:T], AF.Sqrt,
          reads=[("l1a",)], writes=[("l1a",)])
    P.emit("vector", lambda e: e.reciprocal(b.rows[96:112, 0:T], b.rows[32:48, 0:T]),
           reads=[("l1a",)], writes=[("l1ainv",)])
    b.ts(b.rows[64:80, 0:T], b.rows[96:112, 0:T], 27.5, 65.0, ALU.mult, ALU.add,
         reads=[("l1ainv",)], writes=[("l1r",)])
    P.emit("vector", lambda e: e.memset(b.b2[:, 0:4096], 0.0),
           writes=[("l1bufb",)])
    b.dma(b.tfB[:, 0:256], din["c_l1cj"], queue="qsync", writes=[("cj",)])

    rdkeys = [("l1mean",), ("l1a",), ("l1r",)]
    for t in range(TT):
        P.emit("tensor", lambda e, t=t: e.transpose(
            pb[4][:, 0:80], b.rows[0:80, 128 * t:128 * t + 128],
            b.identf[0:80, 0:80]),
            reads=rdkeys + [("identf",)], writes=[("ps", 4)])
        P.emit("vector", lambda e: e.tensor_copy(b.small[:, 224:240], pb[4][:, 0:16]),
               reads=[("ps", 4)], writes=[("l1tc", 0)])
        P.emit("vector", lambda e: e.tensor_copy(b.small[:, 240:256], pb[4][:, 32:48]),
               reads=[("ps", 4)], writes=[("l1tc", 1)])
        P.emit("vector", lambda e: e.tensor_copy(b.small[:, 256:272], pb[4][:, 64:80]),
               reads=[("ps", 4)], writes=[("l1tc", 2)])
        for i, (c0, bo, kk) in enumerate(((224, 0, "l1mb"), (240, 256, "l1ab"),
                                          (256, 512, "l1rb"))):
            b.dma(b.scr64[i:i + 1, 0:2048].rearrange("one (p j) -> (one p) j", j=16),
                  b.small[:, c0:c0 + 16], queue="qout",
                  reads=[("l1tc", i)], writes=[("l1s64", i)])
            b.dma(b.bcA[:, bo:bo + 256],
                  bass.AP(b.scr64[:].tensor, i * 2048, [[16, 128], [0, 16], [1, 16]]),
                  queue="qsync", reads=[("l1s64", i)], writes=[(kk,)])
        b.ts(b.bcA[:, 768:1024], b.bcA[:, 512:768], -1.0, None, ALU.mult,
             reads=[("l1rb",)], writes=[("l1rbn",)])
        b.tt(b.tfB[:, 256:512], b.tfB[:, 0:256], b.bcA[:, 0:256], ALU.subtract,
             reads=[("cj",), ("l1mb",)], writes=[("l1np",)])
        b.tt(b.tfB[:, 256:512], b.tfB[:, 256:512], b.bcA[:, 512:768], ALU.min,
             reads=[("l1np",), ("l1rb",)], writes=[("l1np",)])
        b.tt(b.tfB[:, 256:512], b.tfB[:, 256:512], b.bcA[:, 768:1024], ALU.max,
             reads=[("l1np",), ("l1rbn",)], writes=[("l1np",)])
        b.tt(b.bcA[:, 1280:1536], b.bcA[:, 256:512], b.bcA[:, 256:512], ALU.mult,
             reads=[("l1ab",)], writes=[("l1A2",)])
        b.ts(b.tfB[:, 512:768], b.bcA[:, 1280:1536], -1.0, None, ALU.mult,
             reads=[("l1A2",)], writes=[("l1c1",)])
        b.stt(b.tfB[:, 768:1024], b.bcA[:, 1280:1536], -2.0, b.tfB[:, 256:512],
              ALU.mult, ALU.mult, reads=[("l1A2",), ("l1np",)], writes=[("l1c2",)])
        b.stt(b.bcA[:, 1536:1792], b.tfB[:, 768:1024], 0.5, b.tfB[:, 256:512],
              ALU.mult, ALU.mult, reads=[("l1c2",), ("l1np",)], writes=[("l1c3",)])
        for sap, skey, tys in ((b.tfB[:, 512:768], ("l1c1",), (0, 3)),
                               (b.tfB[:, 768:1024], ("l1c2",), (6,)),
                               (b.bcA[:, 1536:1792], ("l1c3",), (9,))):
            dss = [[bass.AP(b.b2[:].tensor, ty0 + lv,
                            [[NCH * T, 128], [16, 16], [256, 16]])
                    for ty0 in tys] for lv in range(3)]
            prev = sap
            for lv in range(3):
                for dap in dss[lv]:
                    P.emit("vector", lambda e, dap=dap, prev=prev: e.tensor_copy(
                        dap, prev), reads=[skey, ("l1d",)], writes=[("l1bufb",)])
                if lv < 2:
                    P.emit("vector", lambda e, prev=prev, d0=dss[lv][0]: e.tensor_tensor(
                        b.bcA[:, 1024:1280], prev, d0, ALU.subtract),
                        reads=[skey, ("l1d",), ("l1bufb",)], writes=[("l1d",)])
                    prev = b.bcA[:, 1024:1280]
        for k in range(16):
            for blk in range(2):
                sl = 128 * ((2 * k + blk) % 2)
                P.emit("tensor", lambda e, k=k, blk=blk, sl=sl: e.transpose(
                    b.ptr[:, sl:sl + 128],
                    b.b2[:, 256 * k + 128 * blk:256 * k + 128 * blk + 128],
                    b.ident[:]),
                    reads=[("l1bufb",), ("ident",)], writes=[("ptr", sl)])
                P.emit("vector", lambda e, sl=sl: e.tensor_copy(
                    b.b2[:, 4352 + sl:4352 + sl + 128], b.ptr[:, sl:sl + 128]),
                    reads=[("ptr", sl)], writes=[("l1stgb", sl)])
                b.dma(b.h_sp[256 * k + 128 * blk:256 * k + 128 * blk + 128,
                             128 * t:128 * t + 128],
                      b.b2[:, 4352 + sl:4352 + sl + 128], queue="qout",
                      reads=[("l1stgb", sl)], writes=[("l1cf", k, t)])

    P.full_fence()
    for ja in range(min(TT, 8)):
        b.dma(b.b1[:, BAS + 128 * ja:BAS + 128 * ja + 128],
              din["c_l1basis"][128 * ja:128 * ja + 128, :], queue="qsync",
              writes=[("l1bas",)])
    for hgi in range(4):
        b.load_wt_block(din["l1_wval_t"], 256 * hgi, 256, ("wbuf",))
        for tt in range(TT):
            bi = b.bank()
            pw = ("ps", bi)
            for d in range(NCH):
                b.mm(pb[bi][:, 0:256], b.xn16[:, d * T + 128 * tt: d * T + 128 * tt + 128],
                     b.wbuf[:, 512 * d:512 * d + 256],
                     start=(d == 0), stop=(d == NCH - 1),
                     reads=[("wbuf",), ("xn16", d)], writes=[pw])
            dst3 = bass.AP(b.b2[:].tensor, 260 * tt, [[NCH * T, 128], [65, 4], [1, 64]])
            P.emit("vector", lambda e, dst3=dst3, bi=bi: e.tensor_copy(
                dst3, pb[bi][:, 0:256].rearrange("p (h w) -> p h w", w=64)),
                reads=[pw], writes=[("vslL1", tt)])
            ones3 = bass.AP(b.b2[:].tensor, 260 * tt + 64, [[NCH * T, 128], [65, 4], [1, 1]])
            P.emit("vector", lambda e, ones3=ones3: e.memset(ones3, 1.0),
                   reads=(), writes=[("vslL1", tt)])
        for ll in range(4):
            h = 4 * hgi + ll
            b.dma(b.b1[:, SLAB:SLAB + T], b.h_sp[256 * h:256 * h + 128, 0:T],
                  queue="qsync", reads=[("l1cf", h, t2) for t2 in range(TT)],
                  writes=[("l1slab",)])
            if TT > 8:
                b.dma(b.b1[:, SLAB2:SLAB2 + T],
                      b.h_sp[256 * h + 128:256 * h + 256, 0:T],
                      queue="qsync", reads=[("l1cf", h, t2) for t2 in range(TT)],
                      writes=[("l1slab2",)])
            for j in range(TT):
                eo = E0 + T * (j % 2)
                so = SLAB if j < 8 else SLAB2
                sk = ("l1slab",) if j < 8 else ("l1slab2",)
                for half in range(NTB // 2):
                    for pi in range(2):
                        mt = 2 * half + pi
                        b.mm(pb[pi][:], b.b1[:, BAS + 128 * (j % 8):BAS + 128 * (j % 8) + 128],
                             b.b1[:, so + 512 * mt:so + 512 * mt + 512],
                             start=True, stop=True,
                             reads=[("l1bas",), sk], writes=[("ps", pi)])
                    for pi in range(2):
                        mt = 2 * half + pi
                        b.act(b.b1[:, eo + 512 * mt:eo + 512 * mt + 512], pb[pi][:],
                              AF.Exp, reads=[("ps", pi)], writes=[("l1E", j % 2)])
                for mt in range(NTB):
                    b.mm(pb[2 + mt][0:65, :], b.b2[:, 260 * j + 65 * ll:260 * j + 65 * ll + 65],
                         b.b1[:, eo + 512 * mt:eo + 512 * mt + 512],
                         start=(j == 0), stop=(j == TT - 1),
                         reads=[("vslL1", j), ("l1E", j % 2)], writes=[("ps", 2 + mt)])
            for mt in range(NTB):
                P.emit("vector", lambda e, mt=mt: e.tensor_copy(
                    b.rows[64:65, 512 * mt:512 * mt + 512], pb[2 + mt][64:65, :]),
                    reads=[("ps", 2 + mt)], writes=[("z", mt)])
                P.emit("vector", lambda e, mt=mt: e.tensor_copy(
                    b.b2[0:64, CTX + 512 * mt:CTX + 512 * mt + 512], pb[2 + mt][0:64, :]),
                    reads=[("ps", 2 + mt)], writes=[("l1ctx",)])
            P.emit("vector", lambda e: e.reciprocal(b.rows[64:65, 0:T], b.rows[64:65, 0:T]),
                   reads=[("z", mt2) for mt2 in range(NTB)], writes=[("zr",)])
            b.dma(b.scr_rowZ[:, 0:T], b.rows[64:65, 0:T], queue="qout",
                  reads=[("zr",)], writes=[("scrrowZ",)])
            b.dma(b.tfA[0:64, 0:T], b.scr_rowZ[:, 0:T].broadcast_to([64, T]),
                  queue="qsync", reads=[("scrrowZ",)], writes=[("bcZt",)])
            b.tt(b.b2[0:64, CTX:CTX + T], b.b2[0:64, CTX:CTX + T], b.tfA[0:64, 0:T],
                 ALU.mult, reads=[("l1ctx",), ("bcZt",)], writes=[("l1ctx",)])
            b.dma(b.ctx_sp[64 * h:64 * h + 64, 0:T], b.b2[0:64, CTX:CTX + T],
                  queue="qout", reads=[("l1ctx",)], writes=[("ctxsp", h)])

    P.full_fence()
    for d in range(NCH):
        b.dma(b.b2[:, d * T:(d + 1) * T], b.ctx_sp[128 * d:128 * d + 128, 0:T],
              queue="qsync", writes=[("ctx", d)])
    for blk in range(2):
        b.load_wt_block(din["l1_wout_t"], 512 * blk, 512, ("wbuf",))
        for o128 in range(4):
            dch = blk * 4 + o128
            for tb in range(NTB):
                bi = b.bank()
                pw = ("ps", bi)
                for d in range(NCH):
                    b.mm(pb[bi][:], b.wbuf[:, 512 * d + 128 * o128:512 * d + 128 * o128 + 128],
                         b.b2[:, d * T + 512 * tb: d * T + 512 * tb + 512],
                         start=(d == 0), stop=(d == NCH - 1),
                         reads=[("wbuf", d), ("ctx", d)], writes=[pw])
                b.vcopy(b.b1[:, dch * T + 512 * tb: dch * T + 512 * tb + 512],
                        pb[bi][:], reads=[pw], writes=[("s16", dch)])

    if os.environ.get("KERNEL_L1NOFFN"):
        return
    P.full_fence()
    WT2 = T // 4
    for d in range(NCH):
        b.dma(b.ccol[:, 40 + d:41 + d], din["l1_g1_col"][128 * d:128 * d + 128, 0:1],
              queue="qsync", writes=[("l1g1", d)])
    b.ts(b.ccol[:, 40:48], b.ccol[:, 40:48], b.cols[:, 38:39], None, ALU.mult,
         reads=[("l1m1", 38)] + [("l1g1", d2) for d2 in range(NCH)],
         writes=[("l1g1", d2) for d2 in range(NCH)])
    for d in range(NCH):
        b.ts(b.xn16[:, d * T:(d + 1) * T], b.xn16[:, d * T:(d + 1) * T],
             b.ccol[:, 40 + d:41 + d], None, ALU.mult,
             reads=[("xn16", d), ("l1g1", d)], writes=[("xng", d)])
    b.dma(b.small[:, 416:448], din["l1_b1_pc"], queue="qsync", writes=[("l1b1pc",)])
    b.dma(b.small[:, 464:472], din["l1_b1ln_pc"], queue="qsync", writes=[("l1b1pc",)])
    b.vcopy(b.b2[:, 7 * T:7 * T + 8], b.small[:, 464:472],
            reads=[("l1b1pc",)], writes=[("l1bln16",)])
    for mt in range(4):
        for och in range(32):
            ev = och % 2 == 0
            wn = "wbuf" if ev else "wbufB"
            woff = 0 if ev else 2048
            wsl = [b.wbuf[:, woff + 128 * d:woff + 128 * d + 128] for d in range(NCH)]
            b.dma(bass.AP(b.wbuf[:].tensor, woff, [[4096, 128], [128, NCH], [1, 128]]),
                  bass.AP(b.w1q_d.tensor, 128 * och,
                          [[4096, 128], [128 * 4096, NCH], [1, 128]]),
                  queue=("qsync" if ev else "qgp"),
                  writes=[(wn, d) for d in range(NCH)])
            if mt == 0:
                for d in range(NCH):
                    b.mm(pb[5][:, 0:1], wsl[d], b.b2[:, 7 * T + d:7 * T + d + 1],
                         start=(d == 0), stop=(d == NCH - 1),
                         reads=[(wn, d), ("l1bln16",)], writes=[("ps", 5)])
                b.ts(b.small[:, 384 + och:385 + och], pb[5][:, 0:1],
                     b.cols[:, 38:39], None, ALU.mult,
                     reads=[("ps", 5), ("l1m1", 38)], writes=[("l1b1c", och)])
                b.tt(b.small[:, 384 + och:385 + och], b.small[:, 384 + och:385 + och],
                     b.small[:, 416 + och:417 + och], ALU.add,
                     reads=[("l1b1c", och), ("l1b1pc",)], writes=[("l1b1c", och)])
            bi = b.bank()
            pw = ("ps", bi)
            for d in range(NCH):
                b.mm(pb[bi][:, 0:WT2], wsl[d],
                     b.xn16[:, d * T + WT2 * mt: d * T + WT2 * mt + WT2],
                     start=(d == 0), stop=(d == NCH - 1),
                     reads=[(wn, d), ("xng", d)], writes=[pw])
            b.act(b.small16[:, 0:WT2], pb[bi][:, 0:WT2], AF.Gelu,
                  bias=b.small[:, 384 + och:385 + och],
                  reads=[pw, ("l1b1c", och)], writes=[("sm16",)])
            b.dma(b.h_sp[128 * och:128 * och + 128, WT2 * mt:WT2 * mt + WT2],
                  b.small16[:, 0:WT2], queue="qout",
                  reads=[("sm16",)], writes=[("l1hsp", och, mt)])
            b.act(b.small16[:, 512:512 + WT2], b.small16[:, 0:WT2], AF.Square,
                  reads=[("sm16",)], writes=[("sm16",)])
            b.mm(pb[4][0:1, 0:WT2], b.onescol[:], b.small16[:, 0:WT2],
                 start=(och == 0), stop=(och == 31),
                 reads=[("sm16",), ("onescol",)], writes=[("ps", 4)])
            b.mm(pb[4][32:33, 0:WT2], b.onescol[:], b.small16[:, 512:512 + WT2],
                 start=(och == 0), stop=(och == 31),
                 reads=[("sm16",), ("onescol",)], writes=[("ps", 4)])
        P.emit("vector", lambda e, mt=mt: e.tensor_copy(
            b.rows[0:1, WT2 * mt:WT2 * mt + WT2], pb[4][0:1, 0:WT2]),
            reads=[("ps", 4)], writes=[("l1S",)])
        P.emit("vector", lambda e, mt=mt: e.tensor_copy(
            b.rows[32:33, WT2 * mt:WT2 * mt + WT2], pb[4][32:33, 0:WT2]),
            reads=[("ps", 4)], writes=[("l1Q",)])
    b.ts(b.rows[0:1, 0:T], b.rows[0:1, 0:T], 1.0 / 4096.0, None, ALU.mult,
         reads=[("l1S",)], writes=[("l1S",)])
    b.vcopy(b.b2[0:1, 0:T], b.rows[0:1, 0:T], reads=[("l1S",)], writes=[("l1mu16",)])
    b.tt(b.tfB[0:1, 0:T], b.rows[0:1, 0:T], b.rows[0:1, 0:T], ALU.mult,
         reads=[("l1S",)], writes=[("l1mu2",)])
    b.vcopy(b.tfA[0:1, 0:T], b.rows[32:33, 0:T], reads=[("l1Q",)], writes=[("l1r2",)])
    b.stt(b.tfA[0:1, 0:T], b.tfA[0:1, 0:T], 1.0 / 4096.0, b.tfB[0:1, 0:T],
          ALU.mult, ALU.subtract, reads=[("l1r2",), ("l1mu2",)], writes=[("l1r2",)])
    b.act(b.tfA[0:1, 0:T], b.tfA[0:1, 0:T], AF.Ln, bias=b.cols[0:1, 36:37],
          reads=[("l1r2",), ("epscol",)], writes=[("l1r2",)])
    b.act(b.tfA[0:1, 0:T], b.tfA[0:1, 0:T], AF.Exp, scale=-0.5,
          reads=[("l1r2",)], writes=[("l1r2",)])
    b.bcast_row(b.tfA[0:1, 0:T], b.bcA[:, 0:T], T, ("l1r2",), ("l1rB",))
    b.ts(b.bcA[:, 0:T], b.bcA[:, 0:T], b.cols[:, 39:40], None, ALU.mult,
         reads=[("l1rB",), ("l1m1", 39)], writes=[("l1rB",)])
    b.dma(b.tfB[:, 0:32], din["l1_g2_pc"], queue="qsync", writes=[("l1g2pc",)])
    b.dma(b.small[:, 456:464], din["l1_b2_pc"], queue="qsync", writes=[("l1b2pc",)])
    b.dma(b.small[:, 472:504], din["l1_b2ln_pc"], queue="qsync", writes=[("l1b2pc",)])
    b.vcopy(b.b2[:, 7 * T + 8:7 * T + 40], b.small[:, 472:504],
            reads=[("l1b2pc",)], writes=[("l1b2ln16",)])
    for mt in range(4):
        for kc in range(32):
            b.dma(b.xn16[:, WT2 * kc:WT2 * kc + WT2],
                  b.h_sp[128 * kc:128 * kc + 128, WT2 * mt:WT2 * mt + WT2],
                  queue="qsync", reads=[("l1hsp", kc, mt)], writes=[("l1h", kc)])
        for och2 in range(NCH):
            ev = och2 % 2 == 0
            wn = "wbuf" if ev else "wbufC"
            wsl = [b.wbuf[:, 128 * kc:128 * kc + 128] if ev
                   else b.b2[:, T + 128 * kc:T + 128 * kc + 128]
                   for kc in range(32)]
            dstap = (bass.AP(b.wbuf[:].tensor, 0, [[4096, 128], [128, 32], [1, 128]])
                     if ev else
                     bass.AP(b.b2[:].tensor, T, [[NCH * T, 128], [128, 32], [1, 128]]))
            b.dma(dstap,
                  bass.AP(b.w2q_d.tensor, 128 * och2,
                          [[1024, 128], [128 * 1024, 32], [1, 128]]),
                  queue=("qsync" if ev else "qgp"),
                  writes=[(wn, kc) for kc in range(32)])
            if mt == 0:
                for kc in range(32):
                    b.mm(pb[5][:, 0:1], wsl[kc], b.b2[:, 7 * T + 8 + kc:7 * T + 9 + kc],
                         start=(kc == 0), stop=(kc == 31),
                         reads=[(wn, kc), ("l1b2ln16",)], writes=[("ps", 5)])
                b.ts(b.small[:, 448 + och2:449 + och2], pb[5][:, 0:1],
                     b.cols[:, 39:40], None, ALU.mult,
                     reads=[("ps", 5), ("l1m1", 39)], writes=[("l1b2c", och2)])
                b.tt(b.small[:, 448 + och2:449 + och2], b.small[:, 448 + och2:449 + och2],
                     b.small[:, 456 + och2:457 + och2], ALU.add,
                     reads=[("l1b2c", och2), ("l1b2pc",)], writes=[("l1b2c", och2)])
            for kc in range(32):
                b.ts(wsl[kc], wsl[kc], b.tfB[:, kc:kc + 1], None, ALU.mult,
                     reads=[(wn, kc), ("l1g2pc",)], writes=[(wn, kc)])
            if mt == 0:
                for kc in range(32):
                    b.mm(pb[5][0:1, 0:128], b.onescol[:], wsl[kc],
                         start=(kc == 0), stop=(kc == 31),
                         reads=[(wn, kc), ("onescol",)], writes=[("ps", 5)])
                b.ts(b.b2[0:1, 6 * T + 128 * och2:6 * T + 128 * och2 + 128],
                     pb[5][0:1, 0:128], -1.0, None, ALU.mult,
                     reads=[("ps", 5)], writes=[("l1nrow",)])
            bi = b.bank()
            pw = ("ps", bi)
            for kc in range(32):
                b.mm(pb[bi][:, 0:WT2], wsl[kc], b.xn16[:, WT2 * kc:WT2 * kc + WT2],
                     start=(kc == 0), stop=False,
                     reads=[(wn, kc), ("l1h", kc)], writes=[pw])
            b.mm(pb[bi][:, 0:WT2], b.b2[0:1, 6 * T + 128 * och2:6 * T + 128 * och2 + 128],
                 b.b2[0:1, WT2 * mt:WT2 * mt + WT2], start=False, stop=True,
                 reads=[("l1nrow",), ("l1mu16",)], writes=[pw])
            b.tt(b.tfB[:, 1024:1024 + WT2], pb[bi][:, 0:WT2],
                 b.bcA[:, WT2 * mt:WT2 * mt + WT2], ALU.mult,
                 reads=[pw, ("l1rB",)], writes=[("l1o32",)])
            b.act(b.small16[:, 0:WT2], b.tfB[:, 1024:1024 + WT2], AF.Silu,
                  bias=b.small[:, 448 + och2:449 + och2],
                  reads=[("l1o32",), ("l1b2c", och2)], writes=[("sm16",)])
            b.tt(b.tfA[:, 0:WT2], b.b1[:, och2 * T + WT2 * mt: och2 * T + WT2 * mt + WT2],
                 b.small16[:, 0:WT2], ALU.mult,
                 reads=[("s16", och2), ("sm16",)], writes=[("tfA",)])
            b.tt(b.x[:, och2 * T + WT2 * mt: och2 * T + WT2 * mt + WT2],
                 b.x[:, och2 * T + WT2 * mt: och2 * T + WT2 * mt + WT2],
                 b.tfA[:, 0:WT2], ALU.add,
                 reads=[("x", och2), ("tfA",)], writes=[("x", och2)])
